# version 01
# baseline (speedup 1.0000x reference)
import numpy as np
import ml_dtypes
from contextlib import ExitStack
import concourse.bass as bass
import concourse.mybir as mybir
from concourse.bass_utils import run_bass_kernel_spmd

F32 = mybir.dt.float32
BF16 = mybir.dt.bfloat16
I32 = mybir.dt.int32
AF = mybir.ActivationFunctionType
ALU = mybir.AluOpType
AX = mybir.AxisListType

B, L, D = 4, 8192, 1024
E = 2048
DFF = 3072
NOWN = L // 2
NT = NOWN // 128
NIC = DFF // 128
LN_EPS = 1e-5
QS = 1.0 / 32767.0

ENGS = ("sync", "scalar", "vector", "tensor", "gpsimd")
ESEM = {"sync": (None, "ld"), "scalar": ("act", "sst"), "vector": ("dve", None),
        "tensor": ("pe", None), "gpsimd": ("pool", "gst")}


class Slot:
    __slots__ = ("w", "r")

    def __init__(self):
        self.w = None
        self.r = {}


def build(debug=False):
    nc = bass.Bass("TRN2", target_bir_lowering=False)

    x_own = nc.dram_tensor("x_own", [NOWN, D], F32, kind="ExternalInput").ap()
    x_peer = nc.dram_tensor("x_peer", [NOWN, D], F32, kind="ExternalInput").ap()
    w_in = nc.dram_tensor("w_in", [D, 4224], F32, kind="ExternalInput").ap()
    w_out = nc.dram_tensor("w_out", [E, D], F32, kind="ExternalInput").ap()
    w_gate = nc.dram_tensor("w_gate", [DFF, D], F32, kind="ExternalInput").ap()
    w_up = nc.dram_tensor("w_up", [DFF, D], F32, kind="ExternalInput").ap()
    w_down = nc.dram_tensor("w_down", [D, DFF], F32, kind="ExternalInput").ap()
    ln_g_rep = nc.dram_tensor("ln_g_rep", [128, D], F32, kind="ExternalInput").ap()
    ln_b_rep = nc.dram_tensor("ln_b_rep", [128, D], F32, kind="ExternalInput").ap()
    b_in_z = nc.dram_tensor("b_in_z", [128, 1], F32, kind="ExternalInput").ap()
    b_in_uv = nc.dram_tensor("b_in_uv", [1, 4096], F32, kind="ExternalInput").ap()
    b_out_row = nc.dram_tensor("b_out_row", [1, D], F32, kind="ExternalInput").ap()
    gq_i = nc.dram_tensor("gq", [128, 1], F32, kind="ExternalInput").ap()
    bq_i = nc.dram_tensor("bq", [128, 1], F32, kind="ExternalInput").ap()
    gk_i = nc.dram_tensor("gk", [128, 1], F32, kind="ExternalInput").ap()
    bk_i = nc.dram_tensor("bk", [128, 1], F32, kind="ExternalInput").ap()
    ones_f = nc.dram_tensor("ones_f", [1, 128], F32, kind="ExternalInput").ap()
    eps_i = nc.dram_tensor("eps_i", [128, 1], F32, kind="ExternalInput").ap()
    ident_b_i = nc.dram_tensor("ident_b", [128, 128], BF16, kind="ExternalInput").ap()
    ident_f_i = nc.dram_tensor("ident_f", [128, 128], F32, kind="ExternalInput").ap()
    out = nc.dram_tensor("out", [NOWN, D], F32, kind="ExternalOutput").ap()

    u_sc = nc.dram_tensor("u_sc", [NT, 128, E], BF16).ap()
    v_sc = nc.dram_tensor("v_sc", [2 * NT, 128, E], BF16).ap()
    kT_sc = nc.dram_tensor("kT_sc", [128, L], BF16).ap()
    p1_sc = nc.dram_tensor("p1_sc", [NT, 128, E], F32).ap()
    p12_sc = nc.dram_tensor("p12_sc", [NT, 128, E], F32).ap()
    x1_sc = nc.dram_tensor("x1_sc", [NT, 128, D], F32).ap()
    gate_sc = nc.dram_tensor("gate_sc", [NIC, 128, NOWN], F32).ap()
    up_sc = nc.dram_tensor("up_sc", [NIC, 128, NOWN], F32).ap()
    p4_sc = nc.dram_tensor("p4_sc", [NIC, 128, NOWN], F32).ap()
    p5_sc = nc.dram_tensor("p5_sc", [NIC, 128, NOWN], F32).ap()
    flush_dr = nc.dram_tensor("flush_dr", [128, 16], F32).ap()
    cc_in = [nc.dram_tensor(f"cc{i}_in", [8, 1], F32).ap() for i in range(4)]
    cc_out = [nc.dram_tensor(f"cc{i}_out", [8, 1], F32).ap() for i in range(4)]

    dbg = {}
    if debug:
        for nm, shp, dt in [
            ("dbg_kT", [128, L], BF16), ("dbg_qT", [128, NOWN], BF16),
            ("dbg_u0", [128, E], BF16), ("dbg_v0", [128, E], BF16),
            ("dbg_p12", [NT, 128, E], F32), ("dbg_x1", [NT, 128, D], F32),
            ("dbg_gate", [NIC, 128, NOWN], F32), ("dbg_up", [NIC, 128, NOWN], F32),
            ("dbg_p4", [NIC, 128, NOWN], F32), ("dbg_p5", [NIC, 128, NOWN], F32),
            ("dbg_scal", [128, 64], F32), ("dbg_max", [128, 8], F32),
            ("dbg_xg", [128, D], F32), ("dbg_xln", [128, D], F32),
            ("dbg_xnb", [128, D], BF16), ("dbg_zs", [128, 512], F32),
            ("dbg_mrl", [128, 8], F32), ("dbg_stt", [128, 16], F32),
            ("dbg_wdT", [128, NIC, D], BF16),
            ("dbg_wci", [128, D], F32), ("dbg_wqb", [128, D], BF16),
            ("dbg_wsnap", [128, D], F32),
            ("dbg_wTc", [128, 8, 128], BF16), ("dbg_xqT", [128, 8, 512], BF16),
        ]:
            dbg[nm] = nc.dram_tensor(nm, shp, dt, kind="ExternalOutput").ap()

    prog = {e: [] for e in ENGS}
    SEMS = ["ld", "sst", "gst", "pe", "act", "dve", "pool",
            "cc0", "cc1", "cc2", "cc3"]
    ctr = {s: 0 for s in SEMS}
    semh = {}

    def op(eng, f, inc=None, k=1):
        if inc is None:
            prog[eng].append(f)
            return None
        prog[eng].append(lambda e, f=f, s=inc, k=k: f(e).then_inc(semh[s], k))
        ctr[inc] += k
        return ctr[inc]

    def wait(eng, s, v):
        if v and v > 0:
            prog[eng].append(lambda e, s=s, v=int(v): e.wait_ge(semh[s], int(v)))

    def DR(eng="vector"):
        prog[eng].append(lambda e: e.drain())

    OWN = {"sync": ("ld",), "scalar": ("act",), "vector": ("dve",),
           "tensor": ("pe",), "gpsimd": ("pool",)}

    def guard_w(eng, *slots):
        for sl in slots:
            for s, c in sl.r.items():
                if s not in OWN[eng]:
                    wait(eng, s, c)
            if sl.w is not None and sl.w[0] not in OWN[eng]:
                wait(eng, sl.w[0], sl.w[1])

    def guard_r(eng, *slots):
        for sl in slots:
            if sl.w is not None and sl.w[0] not in OWN[eng]:
                wait(eng, sl.w[0], sl.w[1])

    def note_w(sem, cnt, *slots):
        for sl in slots:
            sl.w = (sem, cnt)
            sl.r = {}

    def note_r(sem, cnt, *slots):
        for sl in slots:
            sl.r[sem] = max(sl.r.get(sem, 0), cnt)

    def S(n=1):
        return [Slot() for _ in range(n)] if n > 1 else Slot()

    def seed(ends, *slots):
        for sl in slots:
            for lst in (sl if isinstance(sl, list) else [sl]):
                lst.r = dict(ends)
                lst.w = None

    def ends_now():
        return {s: ctr[s] for s in ("pe", "act", "dve", "pool", "sst", "ld", "gst")
                if ctr[s] > 0}

    def fence():
        for eng, s in (("scalar", "act"), ("vector", "dve"),
                       ("tensor", "pe"), ("gpsimd", "pool")):
            op(eng, lambda e: e.drain(), s)

    SB_LIMIT = 207 * 1024
    big_cm = nc.sbuf_tensor("bigsb", [128, SB_LIMIT], mybir.dt.uint8)
    big_h = big_cm.__enter__()
    big_addr = nc.lookup_mloc(big_h).addr
    _uid = [0]

    def mk_sb(name, shape, dtype, off):
        _uid[0] += 1
        h = nc.virtual_sbuf_tensor(f"{name}_{_uid[0]}", shape, dtype)
        nc._tensor(h.name, list(h.shape), h.dtype, type="SB")
        mloc = nc.lookup_mloc(h)
        mloc.allocated = True
        mloc.addr = big_addr + off
        return h

    class Reg:
        def __init__(self, base):
            self.off = int(base)

        def t(self, name, shape, dtype):
            nbytes = int(np.prod(shape[1:])) * mybir.dt.size(dtype)
            o = self.off
            self.off = (o + nbytes + 31) & ~31
            assert self.off <= SB_LIMIT, f"sbuf overflow at {name}: {self.off}"
            return mk_sb(name, shape, dtype, o)

    ps_cm = nc.psum_tensor("psbig", [128, 4096], F32)
    psB = ps_cm.__enter__()

    def PS(bank, nbanks=1, dt=F32):
        v = psB[:, bank * 512:(bank + nbanks) * 512]
        return v if dt == F32 else v.bitcast(dt)

    per = Reg(0)
    qT = per.t("qT", [128, NOWN], BF16)
    onest = per.t("onest", [1, 128], F32)
    identb = per.t("identb", [128, 128], BF16)
    identf = per.t("identf", [128, 128], F32)
    bz = per.t("bz", [128, 1], F32)
    gqs = per.t("gqs", [128, 4], F32)
    epsc = per.t("epsc", [128, 1], F32)
    st3 = per.t("st3", [128, NT, 2], F32)
    mxa = per.t("mxa", [128, 8], F32)
    scal = per.t("scal", [128, 64], F32)
    tiny = per.t("tiny", [128, 32], F32)
    stt = per.t("stt", [128, 16], F32)
    flsb = per.t("flsb", [128, 16], F32)
    PB = per.off
    sl_qT, sl_st3, sl_mxa, sl_scal, sl_tiny = S(), S(), S(), S(), S()
    sl_stt = S(2)

    C_S1, C_I1 = 0 + 0, 8 + 0
    C_S3, C_I3 = 16 + 2, 24 + 2
    C_S2, C_I2 = 32 + 1, 40 + 1
    C_S4, C_I4 = 32 + 3, 40 + 3
    C_S5, C_I5 = 48 + 4, 56 + 4
    C_C4 = 48 + 0
    C_C45 = 48 + 1

    def ld_dma(dst, src):
        return op("sync", lambda e, d=dst, s=src: e.dma_start(out=d, in_=s), "ld", 16)

    def _fl_norm(b):
        if b.dtype != F32:
            b = b.bitcast(F32)
        nfree = int(np.prod(b.shape[1:]))
        P = int(b.shape[0])
        return b, P, min(nfree, 2)

    def ld_fl(buf2col):
        b, P, n = _fl_norm(buf2col)
        return op("sync", lambda e, b=b, P=P, n=n: e.dma_start(
            out=flush_dr[0:P, 0:n], in_=b[:, 0:n] if b.ndim == 2 else b),
            "ld", 16)

    def st_fl(dram2col, eng="scalar"):
        sem = "sst" if eng == "scalar" else "gst"
        b, P, n = _fl_norm(dram2col)
        return op(eng, lambda e, b=b, P=P, n=n: e.dma_start(
            out=flsb[0:P, 0:n], in_=b[:, 0:n] if b.ndim == 2 else b), sem, 16)

    for dst, src in [(onest[:], ones_f), (identb[:], ident_b_i), (identf[:], ident_f_i),
                     (bz[:], b_in_z), (gqs[:, 0:1], gq_i), (gqs[:, 1:2], bq_i),
                     (gqs[:, 2:3], gk_i), (gqs[:, 3:4], bk_i),
                     (epsc[:], eps_i)]:
        ld_dma(dst, src)
    ld_const = ld_fl(gqs[:, 0:2])
    wait("vector", "ld", ld_const)
    mz = op("vector", lambda e: e.memset(mxa[:], 0.0), "dve")
    note_w("dve", mz, sl_mxa)

    def ln_stats(par, m_dst, r_dst, m_sl, r_sl):
        b0 = par * 8
        DR()
        op("vector", lambda e: e.tensor_scalar(
            stt[:, b0 + 5:b0 + 6], stt[:, b0 + 4:b0 + 5], 1.0 / D, None, op0=ALU.mult))
        DR()
        op("vector", lambda e: e.tensor_tensor(
            stt[:, b0 + 6:b0 + 7], stt[:, b0 + 5:b0 + 6], stt[:, b0 + 5:b0 + 6],
            op=ALU.mult))
        DR()
        op("vector", lambda e: e.scalar_tensor_tensor(
            stt[:, b0 + 7:b0 + 8], stt[:, b0 + 0:b0 + 1], 1.0 / D, stt[:, b0 + 6:b0 + 7],
            op0=ALU.mult, op1=ALU.subtract), "dve")
        v1 = ctr["dve"]
        wait("scalar", "dve", v1)
        op("scalar", lambda e: e.activation(
            stt[:, b0 + 6:b0 + 7], stt[:, b0 + 7:b0 + 8], AF.Sqrt,
            bias=epsc[:], scale=1.0), "act")
        va = ctr["act"]
        wait("vector", "act", va)
        if m_dst is not None:
            op("vector", lambda e: e.tensor_copy(m_dst, stt[:, b0 + 5:b0 + 6]))
        op("vector", lambda e: e.reciprocal(r_dst, stt[:, b0 + 6:b0 + 7]), "dve")
        DR()
        v2 = ctr["dve"]
        note_r("dve", v2, sl_stt[par])
        note_w("dve", v2, m_sl, r_sl)
        return v2

    mloc = per_m = None

    ra = Reg(PB)
    Wbf = ra.t("Wbf", [128, 8, 4224], BF16)
    lgr = ra.t("lgr", [128, D], F32)
    lbr = ra.t("lbr", [128, D], F32)
    buvr = ra.t("buvr", [1, 4096], F32)
    xnT = ra.t("xnT", [128, 2, 8, 512], BF16)
    ut = ra.t("ut", [128, 2, E], BF16)
    vt = ra.t("vt", [128, 2, E], BF16)
    xg = ra.t("xg", [128, 2, 4, D], F32)
    mrl = ra.t("mrl", [128, 8], F32)
    ra_sub = ra.off
    wst = ra.t("wst", [128, 2, 4224], F32)

    sl_Wbf, sl_lgr, sl_lbr, sl_buvr = S(), S(), S(), S()
    sl_xnT, sl_ut, sl_vt, sl_xg = S(2), S(2), S(2), S(2)
    sl_wst = S(2)
    sl_mrl = S(2)

    psA = [PS(0, 1, BF16), PS(1, 1, BF16)]
    psZ = [PS(2), PS(3)]
    psUV = [PS(4), PS(5)]

    ld_dma(lgr[:], ln_g_rep)
    ld_dma(lbr[:], ln_b_rep)
    ld_dma(buvr[:], b_in_uv)
    lf0 = ld_fl(lgr[:, 0:2])
    note_w("ld", lf0, sl_lgr, sl_lbr, sl_buvr)

    for dc in range(8):
        guard_w("sync", sl_wst[dc % 2])
        ld_dma(wst[:, dc % 2, :], w_in[dc * 128:(dc + 1) * 128, :])
        lw = ld_fl(wst[:, dc % 2, 0:2])
        note_w("ld", lw, sl_wst[dc % 2])
        wait("vector", "ld", lw)
        wc = op("vector", lambda e, dc=dc: e.tensor_copy(
            Wbf[:, dc, :], wst[:, dc % 2, :]), "dve")
        note_r("dve", wc, sl_wst[dc % 2])
    note_w("dve", ctr["dve"], sl_Wbf)
    w_cast_done = ctr["dve"]
    rt_a = Reg(ra_sub)
    xlnt = rt_a.t("xlnt", [128, 2, D], F32)
    xnf = rt_a.t("xnf", [128, 2, D], F32)
    xnb = rt_a.t("xnb", [128, 2, D], BF16)
    zs = rt_a.t("zs", [128, 2, 512], F32)
    ktmp = rt_a.t("ktmp", [128, 2, 512], BF16)
    junkA = rt_a.t("junkA", [128, 2, D], F32)
    assert rt_a.off <= SB_LIMIT
    sl_xlnt, sl_xnf, sl_xnb, sl_zs, sl_ktmp = S(2), S(2), S(2), S(2), S(2)
    sl_junkA = S(2)
    seed({"dve": w_cast_done}, sl_xlnt, sl_xnf, sl_xnb, sl_zs, sl_ktmp, sl_junkA)

    xld = [0] * 16
    st_u = [0] * NT
    st_v = [0] * (2 * NT)
    st_kt = [0] * 16
    sl_psA = S(2)
    sl_psZ = S(2)
    sl_psUV = S(2)

    def a_xload(gi):
        src = x_own if gi < 8 else x_peer
        g = gi % 8
        guard_w("sync", sl_xg[gi % 2])
        ld_dma(xg[:, gi % 2, :, :],
               src[g * 512:(g + 1) * 512, :].rearrange("(a p) d -> p a d", p=128))
        xld[gi] = ld_fl(xg[:, gi % 2, 0, 0:2])
        note_w("ld", xld[gi], sl_xg[gi % 2])

    a_xload(0)
    a_xload(1)

    for gi in range(16):
        own = gi < 8
        g = gi % 8
        par = gi % 2
        for a in range(4):
            t_abs = gi * 4 + a
            sp = t_abs % 2
            b0 = sp * 8
            xa = xg[:, par, a, :]
            guard_r("scalar", sl_xg[par])
            guard_w("scalar", sl_junkA[sp])
            sq1 = op("scalar", lambda e, xa=xa, sp=sp: e.activation(
                junkA[:, sp, :], xa, AF.Square), "act")
            note_w("act", sq1, sl_junkA[sp])
            note_r("act", sq1, sl_xg[par])
            guard_r("vector", sl_xg[par])
            guard_w("vector", sl_stt[sp])
            rs1 = op("vector", lambda e, xa=xa, b0=b0: e.tensor_reduce(
                stt[:, b0 + 4:b0 + 5], xa, axis=AX.X, op=ALU.add), "dve")
            note_r("dve", rs1, sl_xg[par])
            wait("vector", "act", sq1)
            rq1 = op("vector", lambda e, sp=sp, b0=b0: e.tensor_reduce(
                stt[:, b0 + 0:b0 + 1], junkA[:, sp, :], axis=AX.X, op=ALU.add), "dve")
            note_r("dve", rq1, sl_junkA[sp])
            lsd = ln_stats(sp, mrl[:, par * 4:par * 4 + 1], mrl[:, par * 4 + 1:par * 4 + 2],
                           sl_mrl[par], sl_mrl[par])
            if debug and gi == 0 and a == 0:
                wait("gpsimd", "dve", ctr["dve"])
                dm = op("gpsimd", lambda e: e.dma_start(
                    out=dbg["dbg_mrl"], in_=mrl[:]), "gst", 16)
                note_r("gst", dm, sl_mrl[0])
                ds = op("gpsimd", lambda e: e.dma_start(
                    out=dbg["dbg_stt"], in_=stt[:]), "gst", 16)
                note_r("gst", ds, sl_stt[0])
            guard_w("vector", sl_xlnt[par])
            xlv = op("vector", lambda e, xa=xa, par=par: e.tensor_scalar(
                xlnt[:, par, :], xa, mrl[:, par * 4:par * 4 + 1],
                mrl[:, par * 4 + 1:par * 4 + 2], op0=ALU.subtract, op1=ALU.mult), "dve")
            note_r("dve", xlv, sl_xg[par], sl_mrl[par])
            note_w("dve", xlv, sl_xlnt[par])
            guard_r("scalar", sl_xlnt[par])
            guard_w("scalar", sl_junkA[sp])
            sq2 = op("scalar", lambda e, par=par, sp=sp: e.activation(
                junkA[:, sp, :], xlnt[:, par, :], AF.Square), "act")
            note_w("act", sq2, sl_junkA[sp])
            note_r("act", sq2, sl_xlnt[par])
            guard_w("vector", sl_stt[sp])
            rs2 = op("vector", lambda e, par=par, b0=b0: e.tensor_reduce(
                stt[:, b0 + 4:b0 + 5], xlnt[:, par, :], axis=AX.X, op=ALU.add), "dve")
            note_r("dve", rs2, sl_xlnt[par])
            wait("vector", "act", sq2)
            rq2 = op("vector", lambda e, sp=sp, b0=b0: e.tensor_reduce(
                stt[:, b0 + 0:b0 + 1], junkA[:, sp, :], axis=AX.X, op=ALU.add), "dve")
            note_r("dve", rq2, sl_junkA[sp])
            ln_stats(sp, mrl[:, par * 4 + 2:par * 4 + 3], mrl[:, par * 4 + 3:par * 4 + 4],
                     sl_mrl[par], sl_mrl[par])
            guard_w("vector", sl_xnf[par])
            xnv = op("vector", lambda e, par=par: e.tensor_scalar(
                xnf[:, par, :], xlnt[:, par, :], mrl[:, par * 4 + 2:par * 4 + 3],
                mrl[:, par * 4 + 3:par * 4 + 4], op0=ALU.subtract, op1=ALU.mult), "dve")
            note_r("dve", xnv, sl_xlnt[par], sl_mrl[par])
            note_w("dve", xnv, sl_xnf[par])
            if debug and gi == 0 and a == 0:
                wait("gpsimd", "ld", xld[0])
                d1 = op("gpsimd", lambda e: e.dma_start(
                    out=dbg["dbg_xg"], in_=xg[:, 0, 0, :]), "gst", 16)
                note_r("gst", d1, sl_xg[0])
                wait("gpsimd", "dve", xlv)
                d2 = op("gpsimd", lambda e: e.dma_start(
                    out=dbg["dbg_xln"], in_=xlnt[:, 0, :]), "gst", 16)
                note_r("gst", d2, sl_xlnt[0])
            guard_r("gpsimd", sl_xnf[par], sl_lgr, sl_lbr)
            op("gpsimd", lambda e, par=par: e.tensor_tensor(
                xnf[:, par, :], xnf[:, par, :], lgr[:], op=ALU.mult))
            guard_w("gpsimd", sl_xnb[par])
            xnbd = op("gpsimd", lambda e, par=par: e.tensor_tensor(
                xnb[:, par, :], xnf[:, par, :], lbr[:], op=ALU.add), "pool")
            note_w("pool", xnbd, sl_xnb[par], sl_xnf[par])
            note_r("pool", xnbd, sl_lgr, sl_lbr)
            prog["gpsimd"].append(lambda e: e.drain())
            if debug and gi == 0 and a == 0:
                d3 = op("gpsimd", lambda e: e.dma_start(
                    out=dbg["dbg_xnb"], in_=xnb[:, 0, :]), "gst", 16)
                note_r("gst", d3, sl_xnb[0])
            par2 = t_abs % 2
            guard_r("tensor", sl_xnb[par])
            guard_w("tensor", sl_psA[par2])
            for dc in range(8):
                op("tensor", lambda e, par=par, par2=par2, dc=dc: e.transpose(
                    psA[par2][:, dc * 128:(dc + 1) * 128],
                    xnb[:, par, dc * 128:(dc + 1) * 128], identb[:]))
            op("tensor", lambda e: e.drain(), "pe")
            trd = ctr["pe"]
            note_r("pe", trd, sl_xnb[par])
            note_w("pe", trd, sl_psA[par2])
            wait("scalar", "pe", trd)
            guard_w("scalar", sl_xnT[par])
            ev = op("scalar", lambda e, par=par, par2=par2, a=a: e.activation(
                xnT[:, par, :, a * 128:(a + 1) * 128],
                psA[par2].rearrange("p (dc c) -> p dc c", c=128),
                AF.Identity), "act")
            note_r("act", ev, sl_psA[par2])
            note_w("act", ev, sl_xnT[par])
            prog["scalar"].append(lambda e: e.drain())
        xnT_ready = ctr["act"]
        if gi + 2 < 16:
            a_xload(gi + 2)

        wait("tensor", "dve", w_cast_done)
        wait("tensor", "act", xnT_ready)
        guard_w("tensor", sl_psZ[par])
        for dc in range(8):
            op("tensor", lambda e, par=par, dc=dc: e.matmul(
                psZ[par], lhsT=Wbf[:, dc, 4096:4224], rhs=xnT[:, par, dc, :],
                start=(dc == 0), stop=(dc == 7)), "pe" if dc == 7 else None)
        zmm = ctr["pe"]
        note_w("pe", zmm, sl_psZ[par])
        note_r("pe", zmm, sl_xnT[par], sl_Wbf)
        wait("scalar", "pe", zmm)
        wait("scalar", "ld", ld_const)
        guard_w("scalar", sl_zs[par])
        zev = op("scalar", lambda e, par=par: e.activation(
            zs[:, par, :], psZ[par], AF.Silu, bias=bz[:]), "act")
        note_r("act", zev, sl_psZ[par])
        note_w("act", zev, sl_zs[par])
        if debug and gi == 0:
            wait("gpsimd", "act", zev)
            d4 = op("gpsimd", lambda e: e.dma_start(
                out=dbg["dbg_zs"], in_=zs[:, 0, :]), "gst", 16)
            note_r("gst", d4, sl_zs[0])
        wait("vector", "act", zev)
        if own:
            qv = op("vector", lambda e, par=par, g=g: e.tensor_scalar(
                qT[:, g * 512:(g + 1) * 512], zs[:, par, :], gqs[:, 0:1], gqs[:, 1:2],
                op0=ALU.mult, op1=ALU.add), "dve")
            note_r("dve", qv, sl_zs[par])
            note_w("dve", qv, sl_qT)
        guard_w("vector", sl_ktmp[par])
        kv = op("vector", lambda e, par=par: e.tensor_scalar(
            ktmp[:, par, :], zs[:, par, :], gqs[:, 2:3], gqs[:, 3:4],
            op0=ALU.mult, op1=ALU.add), "dve")
        note_r("dve", kv, sl_zs[par])
        note_w("dve", kv, sl_ktmp[par])
        wait("scalar", "dve", kv)
        st_kt[gi] = op("scalar", lambda e, par=par, gi=gi: e.dma_start(
            out=kT_sc[:, gi * 512:(gi + 1) * 512], in_=ktmp[:, par, :]), "sst", 16)
        note_r("sst", st_kt[gi], sl_ktmp[par])

        for a in range(4):
            t_abs = gi * 4 + a
            for which in (["u", "v"] if own else ["v"]):
                col0 = 0 if which == "u" else E
                dstb = ut if which == "u" else vt
                sl_dst = sl_ut if which == "u" else sl_vt
                tp = t_abs % 2
                for ec in range(4):
                    pp = (t_abs * 4 + ec) % 2
                    guard_w("tensor", sl_psUV[pp])
                    guard_r("tensor", sl_buvr)
                    op("tensor", lambda e, pp=pp, col0=col0, ec=ec: e.matmul(
                        psUV[pp], lhsT=onest[:],
                        rhs=buvr[:, col0 + ec * 512: col0 + (ec + 1) * 512],
                        start=True, stop=False))
                    for dc in range(8):
                        op("tensor", lambda e, pp=pp, par=par, dc=dc, a=a, col0=col0,
                           ec=ec: e.matmul(
                            psUV[pp], lhsT=xnT[:, par, dc, a * 128:(a + 1) * 128],
                            rhs=Wbf[:, dc, col0 + ec * 512: col0 + (ec + 1) * 512],
                            start=False, stop=(dc == 7)), "pe" if dc == 7 else None)
                    um = ctr["pe"]
                    note_w("pe", um, sl_psUV[pp])
                    note_r("pe", um, sl_xnT[par], sl_Wbf, sl_buvr)
                    wait("scalar", "pe", um)
                    if ec == 0:
                        guard_w("scalar", sl_dst[tp])
                    sev = op("scalar", lambda e, pp=pp, tp=tp, dstb=dstb, ec=ec:
                             e.activation(dstb[:, tp, ec * 512:(ec + 1) * 512],
                                          psUV[pp], AF.Silu), "act")
                    note_r("act", sev, sl_psUV[pp])
                note_w("act", ctr["act"], sl_dst[tp])
                wait("scalar", "act", ctr["act"])
                if which == "u":
                    st_u[t_abs] = op("scalar", lambda e, tp=tp, t=t_abs: e.dma_start(
                        out=u_sc[t], in_=ut[:, tp, :]), "sst", 16)
                    note_r("sst", st_u[t_abs], sl_dst[tp])
                else:
                    st_v[t_abs] = op("scalar", lambda e, tp=tp, t=t_abs: e.dma_start(
                        out=v_sc[t], in_=vt[:, tp, :]), "sst", 16)
                    note_r("sst", st_v[t_abs], sl_dst[tp])

    a_st_flush = st_fl(v_sc[0, :, 0:4])
    fence()
    a_end = {s: ctr[s] for s in ("pe", "act", "dve", "pool")}

    if debug:
        wait("gpsimd", "sst", max(st_kt))
        op("gpsimd", lambda e: e.dma_start(out=dbg["dbg_kT"], in_=kT_sc), "gst", 16)
        wait("gpsimd", "dve", a_end["dve"])
        op("gpsimd", lambda e: e.dma_start(out=dbg["dbg_qT"], in_=qT[:]), "gst", 16)
        wait("gpsimd", "sst", max(st_u))
        op("gpsimd", lambda e: e.dma_start(out=dbg["dbg_u0"], in_=u_sc[0]), "gst", 16)
        wait("gpsimd", "sst", max(st_v))
        op("gpsimd", lambda e: e.dma_start(out=dbg["dbg_v0"], in_=v_sc[0]), "gst", 16)

    rb = Reg(PB)
    v_sb = rb.t("v_sb", [128, 32, E], BF16)
    s2T = rb.t("s2T", [128, 2, 32, 256], BF16)
    kTh = rb.t("kTh", [128, NOWN], BF16)
    rt = rb.t("rt", [128, 2, 256], F32)
    evb = rb.t("evb", [128, 2, 512], F32)
    p1in = rb.t("p1in", [128, 2, 4, 512], F32)
    sl_vsb, sl_kTh = S(), S()
    sl_s2T, sl_rt, sl_evb = S(2), S(2), S(2)
    sl_p1in = S(2)
    seed(ends_now(), sl_vsb, sl_kTh, sl_s2T, sl_rt, sl_evb, sl_p1in)

    psO = [PS(ec) for ec in range(4)]
    psS = [PS(4), PS(5)]
    sl_psS = S(2)
    sl_psO = S()

    NQG = 16
    b_end = {}
    b1_last_st = 0

    for half in range(2):
        is2 = half == 1
        wait("sync", "sst", a_st_flush)
        for s in ("pe", "act", "dve", "pool"):
            wait("sync", s, a_end[s])
        if is2:
            for s in ("pe", "act", "dve"):
                wait("sync", s, b_end[s])
            wait("sync", "sst", b1_last_st)
        guard_w("sync", sl_vsb, sl_kTh)
        for q in range(4):
            ld_dma(v_sb[:, q * 8:(q + 1) * 8, :],
                   v_sc[half * 32 + q * 8: half * 32 + (q + 1) * 8].rearrange(
                       "a p e -> p a e"))
        ld_dma(kTh[:], kTh_src := kT_sc[:, half * NOWN:(half + 1) * NOWN])
        kld = ld_fl(v_sb[:, 0, 0:4])
        note_w("ld", kld, sl_vsb, sl_kTh)
        vld_all = kld

        sq_dve = {}
        beta_pe = {}
        evac_done = {}
        stq_ec = {}
        p1ld = {}

        def emit_alpha(qg, half=half, is2=is2, vld_all=vld_all, kld=kld,
                       sq_dve=None, beta_pe=None):
            pass

        def emit_alpha(qg):
            guard_r("tensor", sl_kTh, sl_qT)
            guard_w("tensor", sl_s2T[qg % 2])
            for kt in range(32):
                sp = kt % 2
                guard_w("tensor", sl_psS[sp])
                mm = op("tensor", lambda e, sp=sp, kt=kt, qg=qg: e.matmul(
                    psS[sp][:, 0:256], lhsT=kTh[:, kt * 128:(kt + 1) * 128],
                    rhs=qT[:, qg * 256:(qg + 1) * 256], start=True, stop=True), "pe")
                note_w("pe", mm, sl_psS[sp])
                if kt == 31:
                    note_r("pe", mm, sl_kTh, sl_qT)
                wait("scalar", "pe", mm)
                guard_w("scalar", sl_rt[sp])
                rl = op("scalar", lambda e, sp=sp: e.activation(
                    rt[:, sp, :], psS[sp][:, 0:256], AF.Relu, scale=2.0 ** -13), "act")
                note_r("act", rl, sl_psS[sp])
                note_w("act", rl, sl_rt[sp])
                wait("vector", "act", rl)
                sq = op("vector", lambda e, sp=sp, kt=kt, qg=qg: e.tensor_tensor(
                    s2T[:, qg % 2, kt, :], rt[:, sp, :], rt[:, sp, :], op=ALU.mult),
                    "dve")
                note_r("dve", sq, sl_rt[sp])
                sq_dve[(qg, kt)] = sq
            note_w("dve", sq_dve[(qg, 31)], sl_s2T[qg % 2])

        def emit_beta(qg, qsub):
            wait("tensor", "dve", sq_dve[(qg, 31)])
            wait("tensor", "ld", vld_all)
            guard_w("tensor", sl_psO)
            for kt in range(32):
                for ec in range(4):
                    op("tensor", lambda e, qg=qg, qsub=qsub, kt=kt, ec=ec: e.matmul(
                        psO[ec],
                        lhsT=s2T[:, qg % 2, kt, qsub * 128:(qsub + 1) * 128],
                        rhs=v_sb[:, kt, ec * 512:(ec + 1) * 512],
                        start=(kt == 0), stop=(kt == 31)),
                        "pe" if (kt == 31 and ec == 3) else None)
            bp = ctr["pe"]
            beta_pe[(qg, qsub)] = bp
            note_w("pe", bp, sl_psO)
            note_r("pe", bp, sl_s2T[qg % 2], sl_vsb)

        def emit_p1load(qg, qsub):
            qt = qg * 2 + qsub
            guard_w("sync", sl_p1in[qsub])
            wait("sync", "sst", b1_stq[qt])
            ld_dma(p1in[:, qsub], p1_sc[qt].rearrange("p (a c) -> p a c", c=512))
            p1ld[(qg, qsub)] = ld_fl(p1in[:, qsub, 0, 0:2])
            note_w("ld", p1ld[(qg, qsub)], sl_p1in[qsub])

        def emit_evac(qg, qsub):
            qt = qg * 2 + qsub
            dst_sc = p1_sc if not is2 else p12_sc
            eng = "scalar" if not is2 else "vector"
            sem = "act" if not is2 else "dve"
            wait(eng, "pe", beta_pe[(qg, qsub)])
            if is2:
                wait(eng, "ld", p1ld[(qg, qsub)])
            for ec in range(4):
                ep = ec % 2
                guard_w(eng, sl_evb[ep])
                if not is2:
                    ee = op(eng, lambda e, ec=ec, ep=ep: e.activation(
                        evb[:, ep, :], psO[ec], AF.Identity), sem)
                else:
                    ee = op(eng, lambda e, ec=ec, ep=ep, qsub=qsub: e.tensor_tensor(
                        evb[:, ep, :], psO[ec], p1in[:, qsub, ec, :], op=ALU.add), sem)
                    note_r(sem, ee, sl_p1in[qsub])
                note_w(sem, ee, sl_evb[ep])
                wait("scalar", sem, ee)
                stc = op("scalar", lambda e, qt=qt, ec=ec, ep=ep, dst_sc=dst_sc:
                         e.dma_start(out=dst_sc[qt, :, ec * 512:(ec + 1) * 512],
                                     in_=evb[:, ep, :]), "sst", 16)
                note_r("sst", stc, sl_evb[ep])
                stq_ec[(qt, ec)] = stc
            note_r(sem, ctr[sem], sl_psO)
            evac_done[(qg, qsub)] = ctr[sem]
            cur_stq[qt] = st_fl(dst_sc[qt, :, 0:2])

        cur_stq = [0] * NT
        if is2:
            emit_p1load(0, 0)
            emit_p1load(0, 1)
        emit_alpha(0)
        for qg in range(NQG):
            emit_beta(qg, 0)
            if qg + 1 < NQG:
                emit_alpha(qg + 1)
            emit_evac(qg, 0)
            emit_beta(qg, 1)
            emit_evac(qg, 1)
            if is2 and qg + 1 < NQG:
                emit_p1load(qg + 1, 0)
                emit_p1load(qg + 1, 1)
        if not is2:
            fence()
            b_end = {s: ctr[s] for s in ("pe", "act", "dve")}
            b1_last_st = cur_stq[NT - 1]
            b1_stq = list(cur_stq)
    fence()
    b2_end = {s: ctr[s] for s in ("pe", "act", "dve")}
    b2_last_st = cur_stq[NT - 1]
    b2_stq = list(cur_stq)

    if debug:
        wait("gpsimd", "sst", b2_last_st)
        op("gpsimd", lambda e: e.dma_start(out=dbg["dbg_p12"], in_=p12_sc), "gst", 16)

    r3 = Reg(PB)
    Wob = r3.t("Wob", [128, 16, D], BF16)
    wost = r3.t("wost", [128, 2, D], F32)
    bor = r3.t("bor", [1, D], F32)
    p12i = r3.t("p12i", [128, 2, E], F32)
    ui = r3.t("ui", [128, 2, E], BF16)
    xi = r3.t("xi", [128, 2, D], F32)
    xl3 = r3.t("xl3", [128, D], F32)
    o2 = r3.t("o2", [128, 2, E], BF16)
    o2T = r3.t("o2T", [128, 2, 16, 128], BF16)
    xpx = r3.t("xpx", [128, 2, D], F32)
    x1t = r3.t("x1t", [128, 2, D], F32)
    tmp3 = r3.t("tmp3", [128, D], F32)
    junk3 = r3.t("junk3", [128, 2, D], F32)
    sl_Wob, sl_bor, sl_xl3, sl_tmp3 = S(), S(), S(), S()
    sl_junk3 = S(2)
    sl_wost, sl_p12i, sl_ui, sl_xi = S(2), S(2), S(2), S(2)
    sl_o2, sl_o2T, sl_xpx, sl_x1t = S(2), S(2), S(2), S(2)
    seed(ends_now(), sl_Wob, sl_bor, sl_xl3, sl_tmp3, sl_junk3, sl_wost,
         sl_p12i, sl_ui, sl_xi, sl_o2, sl_o2T, sl_xpx, sl_x1t)

    psT3 = [PS(0, 2, BF16), PS(2, 2, BF16)]
    psP3 = [PS(4), PS(5)]
    sl_psT3, sl_psP3 = S(2), S(2)

    for s in ("pe", "act", "dve"):
        wait("sync", s, b2_end[s])
    wait("sync", "sst", b2_last_st)
    bl = ld_dma(bor[:], b_out_row)
    note_w("ld", bl, sl_bor)
    for ec in range(16):
        guard_w("sync", sl_wost[ec % 2])
        ld_dma(wost[:, ec % 2, :], w_out[ec * 128:(ec + 1) * 128, :])
        lw = ld_fl(wost[:, ec % 2, 0:2])
        note_w("ld", lw, sl_wost[ec % 2])
        wait("vector", "ld", lw)
        wc = op("vector", lambda e, ec=ec: e.tensor_copy(
            Wob[:, ec, :], wost[:, ec % 2, :]), "dve")
        note_r("dve", wc, sl_wost[ec % 2])
    note_w("dve", ctr["dve"], sl_Wob)

    b3ld = [0] * NT
    b3_x1st = [0] * NT

    def b3_load(t):
        par = t % 2
        guard_w("sync", sl_p12i[par], sl_ui[par], sl_xi[par])
        wait("sync", "sst", b2_stq[t])
        ld_dma(p12i[:, par, :], p12_sc[t])
        ld_dma(ui[:, par, :], u_sc[t])
        ld_dma(xi[:, par, :], x_own[t * 128:(t + 1) * 128, :])
        b3ld[t] = ld_fl(p12i[:, par, 0:2])
        note_w("ld", b3ld[t], sl_p12i[par], sl_ui[par], sl_xi[par])

    b3_load(0)
    b3_load(1)
    for t in range(NT):
        par = t % 2
        sp = t % 2
        b0 = sp * 8
        guard_r("gpsimd", sl_p12i[par], sl_ui[par])
        guard_w("gpsimd", sl_o2[par])
        o2d = op("gpsimd", lambda e, par=par: e.tensor_tensor(
            o2[:, par, :], p12i[:, par, :], ui[:, par, :], op=ALU.mult), "pool")
        note_r("pool", o2d, sl_p12i[par], sl_ui[par])
        note_w("pool", o2d, sl_o2[par])
        prog["gpsimd"].append(lambda e: e.drain())
        guard_r("scalar", sl_xi[par])
        guard_w("scalar", sl_junk3[sp])
        sq1 = op("scalar", lambda e, par=par, sp=sp: e.activation(
            junk3[:, sp, :], xi[:, par, :], AF.Square), "act")
        note_w("act", sq1, sl_junk3[sp])
        note_r("act", sq1, sl_xi[par])
        guard_r("vector", sl_xi[par])
        guard_w("vector", sl_stt[sp])
        rs1 = op("vector", lambda e, par=par, b0=b0: e.tensor_reduce(
            stt[:, b0 + 4:b0 + 5], xi[:, par, :], axis=AX.X, op=ALU.add), "dve")
        note_r("dve", rs1, sl_xi[par])
        wait("vector", "act", sq1)
        rq1 = op("vector", lambda e, sp=sp, b0=b0: e.tensor_reduce(
            stt[:, b0 + 0:b0 + 1], junk3[:, sp, :], axis=AX.X, op=ALU.add), "dve")
        note_r("dve", rq1, sl_junk3[sp])
        ln_stats(sp, tiny[:, 4:5], tiny[:, 5:6], sl_tiny, sl_tiny)
        guard_w("vector", sl_xl3)
        xlv = op("vector", lambda e, par=par: e.tensor_scalar(
            xl3[:], xi[:, par, :], tiny[:, 4:5], tiny[:, 5:6],
            op0=ALU.subtract, op1=ALU.mult), "dve")
        note_r("dve", xlv, sl_xi[par], sl_tiny)
        note_w("dve", xlv, sl_xl3)
        guard_r("gpsimd", sl_xl3, sl_xi[par])
        guard_w("gpsimd", sl_xpx[par])
        xpd = op("gpsimd", lambda e, par=par: e.tensor_tensor(
            xpx[:, par, :], xi[:, par, :], xl3[:], op=ALU.add), "pool")
        note_r("pool", xpd, sl_xl3, sl_xi[par])
        note_w("pool", xpd, sl_xpx[par])
        guard_r("tensor", sl_o2[par])
        guard_w("tensor", sl_psT3[par])
        for ecc in range(16):
            op("tensor", lambda e, par=par, ecc=ecc: e.transpose(
                psT3[par][:, ecc * 128:(ecc + 1) * 128],
                o2[:, par, ecc * 128:(ecc + 1) * 128], identb[:]))
        op("tensor", lambda e: e.drain(), "pe")
        trd = ctr["pe"]
        note_r("pe", trd, sl_o2[par])
        note_w("pe", trd, sl_psT3[par])
        wait("scalar", "pe", trd)
        guard_w("scalar", sl_o2T[par])
        evt = op("scalar", lambda e, par=par: e.activation(
            o2T[:, par, :, :], psT3[par].rearrange("p (a c) -> p a c", c=128),
            AF.Identity), "act")
        note_r("act", evt, sl_psT3[par])
        note_w("act", evt, sl_o2T[par])
        prog["scalar"].append(lambda e: e.drain())
        wait("tensor", "act", evt)
        guard_r("tensor", sl_Wob, sl_bor)
        guard_w("tensor", sl_psP3[par])
        for dch in range(2):
            op("tensor", lambda e, par=par, dch=dch: e.matmul(
                psP3[dch], lhsT=onest[:], rhs=bor[:, dch * 512:(dch + 1) * 512],
                start=True, stop=False))
            for ecc in range(16):
                op("tensor", lambda e, par=par, dch=dch, ecc=ecc: e.matmul(
                    psP3[dch], lhsT=o2T[:, par, ecc, :],
                    rhs=Wob[:, ecc, dch * 512:(dch + 1) * 512],
                    start=False, stop=(ecc == 15)),
                    "pe" if (dch == 1 and ecc == 15) else None)
        pj = ctr["pe"]
        note_w("pe", pj, sl_psP3[par])
        note_r("pe", pj, sl_o2T[par], sl_Wob, sl_bor)
        wait("vector", "pe", pj)
        guard_r("vector", sl_xpx[par])
        guard_w("vector", sl_x1t[par])
        x1d = op("vector", lambda e, par=par: e.tensor_tensor(
            x1t[:, par, :], PS(4, 2),
            xpx[:, par, :], op=ALU.add), "dve")
        note_r("dve", x1d, sl_xpx[par], sl_psP3[par])
        note_w("dve", x1d, sl_x1t[par])
        wait("scalar", "dve", x1d)
        op("scalar", lambda e, par=par, t=t: e.dma_start(
            out=x1_sc[t], in_=x1t[:, par, :]), "sst", 16)
        b3_x1st[t] = st_fl(x1_sc[t, :, 0:2])
        note_r("sst", b3_x1st[t], sl_x1t[par])
        guard_w("scalar", sl_junk3[sp])
        sq3 = op("scalar", lambda e, par=par, sp=sp: e.activation(
            junk3[:, sp, :], x1t[:, par, :], AF.Square), "act")
        note_w("act", sq3, sl_junk3[sp])
        note_r("act", sq3, sl_x1t[par])
        guard_w("vector", sl_stt[sp])
        DR()
        rs3 = op("vector", lambda e, par=par, b0=b0: e.tensor_reduce(
            stt[:, b0 + 4:b0 + 5], x1t[:, par, :], axis=AX.X, op=ALU.add), "dve")
        wait("vector", "act", sq3)
        rq3 = op("vector", lambda e, sp=sp, b0=b0: e.tensor_reduce(
            stt[:, b0 + 0:b0 + 1], junk3[:, sp, :], axis=AX.X, op=ALU.add), "dve")
        note_r("dve", rq3, sl_junk3[sp])
        ln_stats(sp, st3[:, t, 0:1], st3[:, t, 1:2], sl_st3, sl_st3)
        guard_w("vector", sl_tmp3)
        op("vector", lambda e, par=par, t=t: e.tensor_scalar(
            tmp3[:], x1t[:, par, :], st3[:, t, 0:1], None, op0=ALU.subtract))
        DR()
        op("vector", lambda e: e.tensor_reduce(
            tiny[:, 6:7], tmp3[:], axis=AX.X, op=ALU.max, apply_absolute_value=True))
        DR()
        op("vector", lambda e, t=t: e.tensor_tensor(
            tiny[:, 7:8], tiny[:, 6:7], st3[:, t, 1:2], op=ALU.mult))
        DR()
        mxd = op("vector", lambda e: e.tensor_tensor(
            mxa[:, 0:1], mxa[:, 0:1], tiny[:, 7:8], op=ALU.max), "dve")
        note_r("dve", mxd, sl_x1t[par], sl_st3)
        note_w("dve", mxd, sl_mxa, sl_tmp3, sl_tiny)
        if t + 2 < NT:
            b3_load(t + 2)

    fence()
    b3_end = {s: ctr[s] for s in ("pe", "act", "dve", "pool")}

    if debug:
        wait("gpsimd", "sst", b3_x1st[NT - 1])
        op("gpsimd", lambda e: e.dma_start(out=dbg["dbg_x1"], in_=x1_sc), "gst", 16)

    psAR = PS(6)
    sl_psAR = S()

    def emit_allreduce(idx):
        base = 16 * idx
        ccs = f"cc{idx}"
        guard_r("tensor", sl_mxa)
        guard_w("tensor", sl_psAR)
        op("tensor", lambda e: e.transpose(
            psAR[0:8, 0:128], mxa[:, 0:8], identf[:]))
        tr1 = op("tensor", lambda e: e.drain(), "pe")
        note_r("pe", tr1, sl_mxa)
        note_w("pe", tr1, sl_psAR)
        wait("vector", "pe", tr1)
        guard_w("vector", sl_tiny)
        rd = op("vector", lambda e: e.tensor_reduce(
            tiny[0:8, 0:1], psAR[0:8, 0:128], axis=AX.X, op=ALU.max), "dve")
        note_r("dve", rd, sl_psAR)
        note_w("dve", rd, sl_tiny)
        wait("gpsimd", "dve", rd)
        op("gpsimd", lambda e, idx=idx: e.dma_start(
            out=cc_in[idx], in_=tiny[0:8, 0:1]), "gst", 16)
        g0 = op("gpsimd", lambda e, idx=idx: e.dma_start(
            out=flsb[0:8, 2:3], in_=cc_in[idx]), "gst", 16)
        note_r("gst", g0, sl_tiny)
        wait("gpsimd", "gst", g0)
        op("gpsimd", lambda e, idx=idx: e.collective_compute(
            "AllReduce", ALU.max, replica_groups=[list(range(8))],
            ins=[cc_in[idx].opt()], outs=[cc_out[idx].opt()]), ccs, 1)
        wait("gpsimd", ccs, 1)
        op("gpsimd", lambda e, idx=idx: e.dma_start(
            out=tiny[0:8, 1:2], in_=cc_out[idx]), "gst", 16)
        g1 = op("gpsimd", lambda e: e.dma_start(
            out=flush_dr[0:8, 2:4], in_=tiny[0:8, 0:2]), "gst", 16)
        note_w("gst", g1, sl_tiny)
        wait("vector", "gst", g1)
        op("vector", lambda e: e.tensor_scalar(
            tiny[0:8, 2:3], tiny[0:8, 1:2], QS, 1e-12, op0=ALU.mult, op1=ALU.max))
        DR()
        sd = op("vector", lambda e: e.reciprocal(tiny[0:8, 3:4], tiny[0:8, 2:3]), "dve")
        note_w("dve", sd, sl_tiny)
        wait("tensor", "dve", sd)
        op("tensor", lambda e: e.transpose(
            psAR[0:1, 0:8], tiny[0:8, 2:3], identf[0:8, 0:8]))
        op("tensor", lambda e: e.transpose(
            psAR[0:1, 8:16], tiny[0:8, 3:4], identf[0:8, 0:8]))
        tr2 = op("tensor", lambda e: e.drain(), "pe")
        note_r("pe", tr2, sl_tiny)
        note_w("pe", tr2, sl_psAR)
        wait("vector", "pe", tr2)
        cp = op("vector", lambda e: e.tensor_copy(
            tiny[0:1, 16:32], psAR[0:1, 0:16]), "dve")
        note_r("dve", cp, sl_psAR)
        note_w("dve", cp, sl_tiny)
        wait("tensor", "dve", cp)
        guard_w("tensor", sl_psAR)
        bc = op("tensor", lambda e: e.matmul(
            psAR[:, 0:16], lhsT=onest[:], rhs=tiny[0:1, 16:32],
            start=True, stop=True), "pe")
        note_r("pe", bc, sl_tiny)
        note_w("pe", bc, sl_psAR)
        wait("vector", "pe", bc)
        guard_w("vector", sl_scal)
        fin = op("vector", lambda e, base=base: e.tensor_copy(
            scal[:, base:base + 16], psAR[:, 0:16]), "dve")
        note_r("dve", fin, sl_psAR)
        note_w("dve", fin, sl_scal)
        return fin

    ar0_done = emit_allreduce(0)

    fence()
    rw = Reg(PB)
    wdT = rw.t("wdT", [128, NIC, D], BF16)
    sl_wdT = S()
    seed(ends_now(), sl_wdT)
    PC = rw.off

    r4 = Reg(PC)
    xqT = r4.t("xqT", [128, 8, NOWN], BF16)
    x1i = r4.t("x1i", [128, 2, D], F32)
    xqn = r4.t("xqn", [128, D], I32)
    xqb = r4.t("xqb", [128, D], BF16)
    wci = r4.t("wci", [128, 2, D], F32)
    wtm = r4.t("wtm", [128, D], F32)
    wni = r4.t("wni", [128, D], I32)
    wcl = r4.t("wcl", [128, D], F32)
    wqb = r4.t("wqb", [128, 2, D], BF16)
    wTc = r4.t("wTc", [128, 2, 8, 128], BF16)
    bsx = r4.t("bsx", [128, 3, 32], F32)
    wsnap = r4.t("wsnap", [128, D], F32)
    gte = r4.t("gte", [128, 2, 512], F32)
    sl_xqT, sl_xqn, sl_xqb = S(), S(), S()
    sl_x1i, sl_wci, sl_wqb, sl_wTc, sl_gte = S(2), S(2), S(2), S(2), S(2)
    sl_wtm, sl_wni, sl_wcl, sl_bsx = S(), S(), S(), S()
    seed(ends_now(), sl_xqT, sl_xqn, sl_xqb, sl_x1i, sl_wci, sl_wqb, sl_wTc,
         sl_gte, sl_wtm, sl_wni, sl_wcl, sl_bsx)

    psX = [PS(0, 1, BF16), PS(1, 1, BF16)]
    psC = [PS(2), PS(3)]
    sl_psX, sl_psC = S(2), S(2)

    for s in ("pe", "act", "dve", "pool"):
        wait("sync", s, b3_end[s])

    for t in range(NT):
        par = t % 2
        guard_w("sync", sl_x1i[par])
        wait("sync", "sst", b3_x1st[t])
        ld_dma(x1i[:, par, :], x1_sc[t])
        xl = ld_fl(x1i[:, par, 0:2])
        note_w("ld", xl, sl_x1i[par])
        wait("vector", "ld", xl)
        wait("vector", "dve", ar0_done)
        guard_r("vector", sl_st3, sl_scal)
        guard_w("vector", sl_tiny, sl_xqn, sl_xqb)
        op("vector", lambda e, t=t: e.tensor_tensor(
            tiny[:, 8:9], st3[:, t, 1:2], scal[:, C_I1:C_I1 + 1], op=ALU.mult))
        DR()
        op("vector", lambda e, par=par, t=t: e.tensor_scalar(
            xqn[:], x1i[:, par, :], st3[:, t, 0:1], tiny[:, 8:9],
            op0=ALU.subtract, op1=ALU.mult))
        DR()
        xqd = op("vector", lambda e: e.tensor_scalar(
            xqb[:], xqn[:], scal[:, C_S1:C_S1 + 1], None, op0=ALU.mult), "dve")
        note_r("dve", xqd, sl_x1i[par], sl_st3, sl_scal)
        note_w("dve", xqd, sl_xqn, sl_xqb, sl_tiny)
        tp = t % 2
        guard_r("tensor", sl_xqb)
        guard_w("tensor", sl_psX[tp])
        for dc in range(8):
            op("tensor", lambda e, tp=tp, dc=dc: e.transpose(
                psX[tp][:, dc * 128:(dc + 1) * 128],
                xqb[:, dc * 128:(dc + 1) * 128], identb[:]))
        op("tensor", lambda e: e.drain(), "pe")
        trx = ctr["pe"]
        note_r("pe", trx, sl_xqb)
        note_w("pe", trx, sl_psX[tp])
        wait("scalar", "pe", trx)
        guard_w("scalar", sl_xqT)
        evq = op("scalar", lambda e, tp=tp, t=t: e.activation(
            xqT[:, :, t * 128:(t + 1) * 128],
            psX[tp].rearrange("p (a c) -> p a c", c=128), AF.Identity), "act")
        note_r("act", evq, sl_psX[tp])
        prog["scalar"].append(lambda e: e.drain())
    note_w("act", ctr["act"], sl_xqT)
    xq_ready = ctr["act"]

    mm_st = {}

    def c2_matmul_phase(w_src, dst_sc, mx_col, phase):
        for ic in range(NIC):
            wp = ic % 2
            guard_w("sync", sl_wci[wp])
            ld_dma(wci[:, wp, :], w_src[ic * 128:(ic + 1) * 128, :])
            lw = ld_fl(wci[:, wp, 0:2])
            note_w("ld", lw, sl_wci[wp])
            wait("vector", "ld", lw)
            if debug and phase == 0 and ic == 0:
                snp = op("vector", lambda e: e.tensor_copy(wsnap[:], wci[:, 0, :]),
                         "dve")
                prog["vector"].append(lambda e: e.drain())
                wait("gpsimd", "dve", snp)
                dsn = op("gpsimd", lambda e: e.dma_start(
                    out=dbg["dbg_wsnap"], in_=wsnap[:]), "gst", 16)
            guard_w("vector", sl_bsx)
            op("vector", lambda e, wp=wp: e.tensor_reduce(
                bsx[:, 0, :], wci[:, wp, :].rearrange("p (a b) -> p a b", b=32),
                axis=AX.X, op=ALU.max, apply_absolute_value=True))
            DR()
            op("vector", lambda e: e.tensor_scalar(
                bsx[:, 1, :], bsx[:, 0, :], 1.0 / 7.0, 1e-12, op0=ALU.mult, op1=ALU.max))
            DR()
            sd = op("vector", lambda e: e.reciprocal(bsx[:, 2, :], bsx[:, 1, :]), "dve")
            note_r("dve", sd, sl_wci[wp])
            note_w("dve", sd, sl_bsx)
            guard_r("vector", sl_wci[wp])
            guard_w("vector", sl_wtm, sl_wni, sl_wcl)
            DR()
            op("vector", lambda e, wp=wp: e.tensor_tensor(
                wtm[:].rearrange("p (a b) -> p a b", b=32),
                wci[:, wp, :].rearrange("p (a b) -> p a b", b=32),
                bsx[:, 2, :].rearrange("p (a o) -> p a o", o=1).broadcast_to(
                    [128, 32, 32]), op=ALU.mult))
            DR()
            op("vector", lambda e: e.tensor_scalar(wni[:], wtm[:], 1.0, None,
                                                   op0=ALU.mult))
            DR()
            op("vector", lambda e: e.tensor_scalar(
                wcl[:], wni[:], 7.0, -8.0, op0=ALU.min, op1=ALU.max))
            DR()
            qd = op("vector", lambda e, wp=wp: e.tensor_tensor(
                wqb[:, wp, :].rearrange("p (a b) -> p a b", b=32),
                wcl[:].rearrange("p (a b) -> p a b", b=32),
                bsx[:, 1, :].rearrange("p (a o) -> p a o", o=1).broadcast_to(
                    [128, 32, 32]), op=ALU.mult), "dve")
            note_r("dve", qd, sl_wci[wp], sl_bsx, sl_wtm, sl_wcl)
            note_w("dve", qd, sl_wqb[wp], sl_wtm, sl_wni, sl_wcl)
            DR()
            if debug and phase == 0 and ic == 0:
                wait("gpsimd", "ld", lw)
                dw1 = op("gpsimd", lambda e: e.dma_start(
                    out=dbg["dbg_wci"], in_=wci[:, 0, :]), "gst", 16)
                note_r("gst", dw1, sl_wci[0])
                wait("gpsimd", "dve", qd)
                dw2 = op("gpsimd", lambda e: e.dma_start(
                    out=dbg["dbg_wqb"], in_=wqb[:, 0, :]), "gst", 16)
                note_r("gst", dw2, sl_wqb[0])
            guard_r("tensor", sl_wqb[wp])
            guard_w("tensor", sl_psX[wp])
            for dc in range(8):
                op("tensor", lambda e, wp=wp, dc=dc: e.transpose(
                    psX[wp][:, dc * 128:(dc + 1) * 128],
                    wqb[:, wp, dc * 128:(dc + 1) * 128], identb[:]))
            op("tensor", lambda e: e.drain(), "pe")
            trw = ctr["pe"]
            note_r("pe", trw, sl_wqb[wp])
            note_w("pe", trw, sl_psX[wp])
            wait("scalar", "pe", trw)
            guard_w("scalar", sl_wTc[wp])
            evw = op("scalar", lambda e, wp=wp: e.activation(
                wTc[:, wp, :, :], psX[wp].rearrange("p (a c) -> p a c", c=128),
                AF.Identity), "act")
            note_r("act", evw, sl_psX[wp])
            note_w("act", evw, sl_wTc[wp])
            prog["scalar"].append(lambda e: e.drain())
            if debug and phase == 0 and ic == 0:
                wait("gpsimd", "act", evw)
                dw3 = op("gpsimd", lambda e: e.dma_start(
                    out=dbg["dbg_wTc"], in_=wTc[:, 0, :, :]), "gst", 16)
                note_r("gst", dw3, sl_wTc[0])
                dw4 = op("gpsimd", lambda e: e.dma_start(
                    out=dbg["dbg_xqT"], in_=xqT[:, :, 0:512]), "gst", 16)
                note_r("gst", dw4, sl_xqT)
            wait("tensor", "act", evw)
            if phase == 0 and ic == 0:
                wait("tensor", "act", xq_ready)
            for g in range(8):
                gp = (ic * 8 + g) % 2
                guard_w("tensor", sl_psC[gp])
                for dc in range(8):
                    op("tensor", lambda e, wp=wp, gp=gp, dc=dc, g=g: e.matmul(
                        psC[gp], lhsT=wTc[:, wp, dc, :],
                        rhs=xqT[:, dc, g * 512:(g + 1) * 512],
                        start=(dc == 0), stop=(dc == 7)), "pe" if dc == 7 else None)
                gmm = ctr["pe"]
                note_w("pe", gmm, sl_psC[gp])
                note_r("pe", gmm, sl_wTc[wp], sl_xqT)
                wait("scalar", "pe", gmm)
                guard_w("scalar", sl_gte[gp])
                gev = op("scalar", lambda e, gp=gp: e.activation(
                    gte[:, gp, :], psC[gp], AF.Identity), "act")
                note_r("act", gev, sl_psC[gp])
                note_w("act", gev, sl_gte[gp])
                wait("scalar", "act", gev)
                mm_st[(phase, ic * 8 + g)] = op(
                    "scalar", lambda e, gp=gp, ic=ic, g=g, dst_sc=dst_sc: e.dma_start(
                        out=dst_sc[ic, :, g * 512:(g + 1) * 512], in_=gte[:, gp, :]),
                    "sst", 16)
                note_r("sst", mm_st[(phase, ic * 8 + g)], sl_gte[gp])
                if g == 7:
                    mm_st[(phase, ic * 8 + 7)] = st_fl(dst_sc[ic, :, 0:2])
                wait("vector", "act", gev)
                guard_w("vector", sl_stt[0])
                op("vector", lambda e, gp=gp: e.tensor_reduce(
                    tiny[:, 6:7], gte[:, gp, :], axis=AX.X, op=ALU.max,
                    apply_absolute_value=True))
                DR()
                guard_w("vector", sl_mxa)
                rmx = op("vector", lambda e, mx_col=mx_col: e.tensor_tensor(
                    mxa[:, mx_col:mx_col + 1], mxa[:, mx_col:mx_col + 1],
                    tiny[:, 6:7], op=ALU.max), "dve")
                note_r("dve", rmx, sl_gte[gp])
                note_w("dve", rmx, sl_mxa, sl_tiny)

    c2_matmul_phase(w_gate, gate_sc, 2, 0)
    ar1_done = emit_allreduce(1)
    guard_w("vector", sl_scal)
    DR()
    c4d = op("vector", lambda e: e.tensor_scalar(
        scal[:, C_C4:C_C4 + 1], scal[:, C_S3:C_S3 + 1], 2.0 ** -16, None,
        op0=ALU.mult), "dve")
    note_w("dve", c4d, sl_scal)

    c2_matmul_phase(w_up, up_sc, 1, 1)
    up_end_dve = ctr["dve"]

    for dc2 in range(8):
        wp = dc2 % 2
        guard_w("sync", sl_wci[wp])
        for cs in range(3):
            guard_w("sync", sl_wci[wp])
            ld_dma(wci[:, wp, :],
                   w_down[dc2 * 128:(dc2 + 1) * 128, cs * 1024:(cs + 1) * 1024])
            lw = ld_fl(wci[:, wp, 0:2])
            note_w("ld", lw, sl_wci[wp])
            wait("vector", "ld", lw)
            if False:
                snp = op("vector", lambda e: e.tensor_copy(wsnap[:], wci[:, 0, :]),
                         "dve")
                prog["vector"].append(lambda e: e.drain())
                wait("gpsimd", "dve", snp)
                dsn = op("gpsimd", lambda e: e.dma_start(
                    out=dbg["dbg_wsnap"], in_=wsnap[:]), "gst", 16)
            guard_w("vector", sl_bsx)
            op("vector", lambda e, wp=wp: e.tensor_reduce(
                bsx[:, 0, :], wci[:, wp, :].rearrange("p (a b) -> p a b", b=32),
                axis=AX.X, op=ALU.max, apply_absolute_value=True))
            DR()
            op("vector", lambda e: e.tensor_scalar(
                bsx[:, 1, :], bsx[:, 0, :], 1.0 / 7.0, 1e-12,
                op0=ALU.mult, op1=ALU.max))
            DR()
            sd = op("vector", lambda e: e.reciprocal(bsx[:, 2, :], bsx[:, 1, :]), "dve")
            note_r("dve", sd, sl_wci[wp])
            note_w("dve", sd, sl_bsx)
            guard_r("vector", sl_wci[wp])
            guard_w("vector", sl_wtm, sl_wni, sl_wcl)
            DR()
            op("vector", lambda e, wp=wp: e.tensor_tensor(
                wtm[:].rearrange("p (a b) -> p a b", b=32),
                wci[:, wp, :].rearrange("p (a b) -> p a b", b=32),
                bsx[:, 2, :].rearrange("p (a o) -> p a o", o=1).broadcast_to(
                    [128, 32, 32]), op=ALU.mult))
            DR()
            op("vector", lambda e: e.tensor_scalar(wni[:], wtm[:], 1.0, None,
                                                   op0=ALU.mult))
            DR()
            op("vector", lambda e: e.tensor_scalar(
                wcl[:], wni[:], 7.0, -8.0, op0=ALU.min, op1=ALU.max))
            DR()
            qd = op("vector", lambda e, wp=wp: e.tensor_tensor(
                wqb[:, wp, :].rearrange("p (a b) -> p a b", b=32),
                wcl[:].rearrange("p (a b) -> p a b", b=32),
                bsx[:, 1, :].rearrange("p (a o) -> p a o", o=1).broadcast_to(
                    [128, 32, 32]), op=ALU.mult), "dve")
            note_r("dve", qd, sl_wci[wp], sl_bsx, sl_wtm, sl_wcl)
            note_w("dve", qd, sl_wqb[wp], sl_wtm, sl_wni, sl_wcl)
            DR()
            guard_r("tensor", sl_wqb[wp])
            guard_w("tensor", sl_psX[wp])
            for j in range(8):
                op("tensor", lambda e, wp=wp, j=j: e.transpose(
                    psX[wp][:, j * 128:(j + 1) * 128],
                    wqb[:, wp, j * 128:(j + 1) * 128], identb[:]))
            op("tensor", lambda e: e.drain(), "pe")
            trw = ctr["pe"]
            note_r("pe", trw, sl_wqb[wp])
            note_w("pe", trw, sl_psX[wp])
            wait("scalar", "pe", trw)
            guard_w("scalar", sl_wdT)
            evw = op("scalar", lambda e, wp=wp, cs=cs, dc2=dc2: e.activation(
                wdT[:, cs * 8:(cs + 1) * 8, dc2 * 128:(dc2 + 1) * 128],
                psX[wp].rearrange("p (a c) -> p a c", c=128), AF.Identity),
                "act")
            note_r("act", evw, sl_psX[wp])
            note_w("act", evw, sl_wdT)
            prog["scalar"].append(lambda e: e.drain())
    wd_ready = ctr["act"]

    fence()
    r5 = Reg(PC)
    gin = r5.t("gin", [128, 2, 2048], F32)
    upin = r5.t("upin", [128, 2, 2048], F32)
    tA = r5.t("tA", [128, 2048], I32)
    tB = r5.t("tB", [128, 2048], F32)
    tC = r5.t("tC", [128, 2048], F32)
    tDi = r5.t("tDi", [128, 2048], I32)
    p4t = r5.t("p4t", [128, 2, 2048], F32)
    sl_gin, sl_upin, sl_p4t = S(2), S(2), S(2)
    sl_tA, sl_tB, sl_tC, sl_tDi = S(), S(), S(), S()
    seed(ends_now(), sl_gin, sl_upin, sl_p4t, sl_tA, sl_tB, sl_tC, sl_tDi)

    NCH = NIC * 2
    c3_st = [0] * NCH

    for ch in range(NCH):
        ic, cs = ch // 2, ch % 2
        par = ch % 2
        guard_w("sync", sl_gin[par])
        wait("sync", "sst", mm_st[(0, ic * 8 + 7)])
        ld_dma(gin[:, par, :], gate_sc[ic, :, cs * 2048:(cs + 1) * 2048])
        gl = ld_fl(gin[:, par, 0:2])
        note_w("ld", gl, sl_gin[par])
        wait("vector", "ld", gl)
        if ch == 0:
            wait("vector", "dve", ar1_done)
        guard_r("vector", sl_scal)
        guard_w("vector", sl_tA)
        n3d = op("vector", lambda e, par=par: e.tensor_scalar(
            tA[:], gin[:, par, :], scal[:, C_I3:C_I3 + 1], None, op0=ALU.mult), "dve")
        note_r("dve", n3d, sl_gin[par], sl_scal)
        note_w("dve", n3d, sl_tA)
        guard_r("gpsimd", sl_tA, sl_scal)
        guard_w("gpsimd", sl_tB)
        g1d = op("gpsimd", lambda e: e.tensor_scalar(
            tB[:], tA[:], scal[:, C_S3:C_S3 + 1], None, op0=ALU.mult), "pool")
        note_r("pool", g1d, sl_tA, sl_scal)
        note_w("pool", g1d, sl_tB)
        wait("scalar", "pool", g1d)
        guard_w("scalar", sl_tC)
        sgd = op("scalar", lambda e: e.activation(tC[:], tB[:], AF.Sigmoid), "act")
        note_r("act", sgd, sl_tB)
        note_w("act", sgd, sl_tC)
        wait("vector", "act", sgd)
        guard_w("vector", sl_tDi, sl_tB)
        op("vector", lambda e: e.tensor_scalar(tB[:], tC[:], 65536.0, 65535.0,
                                               op0=ALU.mult, op1=ALU.min))
        DR()
        op("vector", lambda e: e.tensor_scalar(tDi[:], tB[:], 1.0, None,
                                               op0=ALU.mult))
        DR()
        op("vector", lambda e: e.tensor_tensor(tC[:], tA[:], tDi[:], op=ALU.mult))
        DR()
        guard_w("vector", sl_stt[0])
        op("vector", lambda e: e.tensor_reduce(
            tiny[:, 6:7], tC[:], axis=AX.X, op=ALU.max, apply_absolute_value=True))
        DR()
        op("vector", lambda e: e.tensor_tensor(
            tiny[:, 7:8], tiny[:, 6:7], scal[:, C_C4:C_C4 + 1], op=ALU.mult))
        DR()
        guard_w("vector", sl_mxa)
        t4d = op("vector", lambda e: e.tensor_tensor(
            mxa[:, 3:4], mxa[:, 3:4], tiny[:, 7:8], op=ALU.max), "dve")
        note_r("dve", t4d, sl_tA)
        note_w("dve", t4d, sl_tB, sl_tC, sl_tDi, sl_mxa, sl_tiny)
        guard_r("gpsimd", sl_tC, sl_scal)
        guard_w("gpsimd", sl_p4t[par])
        wait("gpsimd", "dve", c4d)
        p4d = op("gpsimd", lambda e, par=par: e.tensor_scalar(
            p4t[:, par, :], tC[:], scal[:, C_C4:C_C4 + 1], None, op0=ALU.mult), "pool")
        note_r("pool", p4d, sl_tC, sl_scal)
        note_w("pool", p4d, sl_p4t[par])
        wait("scalar", "pool", p4d)
        op("scalar", lambda e, par=par, ic=ic, cs=cs: e.dma_start(
            out=p4_sc[ic, :, cs * 2048:(cs + 1) * 2048], in_=p4t[:, par, :]), "sst", 16)
        c3_st[ch] = st_fl(p4_sc[ic, :, cs * 2048:cs * 2048 + 2])
        note_r("sst", c3_st[ch], sl_p4t[par])

    wait("vector", "dve", up_end_dve)
    ar2_done = emit_allreduce(2)
    guard_w("vector", sl_scal)
    DR()
    c45d = op("vector", lambda e: e.tensor_tensor(
        scal[:, C_C45:C_C45 + 1], scal[:, C_S4:C_S4 + 1], scal[:, C_S2:C_S2 + 1],
        op=ALU.mult), "dve")
    note_w("dve", c45d, sl_scal)

    c4_st = [0] * NCH
    for ch in range(NCH):
        ic, cs = ch // 2, ch % 2
        par = ch % 2
        guard_w("sync", sl_gin[par], sl_upin[par])
        wait("sync", "sst", c3_st[ch])
        wait("sync", "sst", mm_st[(1, ic * 8 + 7)])
        ld_dma(gin[:, par, :], p4_sc[ic, :, cs * 2048:(cs + 1) * 2048])
        ld_dma(upin[:, par, :], up_sc[ic, :, cs * 2048:(cs + 1) * 2048])
        ul = ld_fl(gin[:, par, 0:2])
        note_w("ld", ul, sl_gin[par], sl_upin[par])
        wait("vector", "ld", ul)
        if ch == 0:
            wait("vector", "dve", c45d)
        guard_r("vector", sl_scal)
        guard_w("vector", sl_tA, sl_tDi, sl_tB)
        op("vector", lambda e, par=par: e.tensor_scalar(
            tA[:], gin[:, par, :], scal[:, C_I4:C_I4 + 1], None, op0=ALU.mult))
        op("vector", lambda e, par=par: e.tensor_scalar(
            tDi[:], upin[:, par, :], scal[:, C_I2:C_I2 + 1], None, op0=ALU.mult))
        DR()
        op("vector", lambda e: e.tensor_tensor(tB[:], tA[:], tDi[:], op=ALU.mult))
        DR()
        guard_w("vector", sl_stt[0])
        op("vector", lambda e: e.tensor_reduce(
            tiny[:, 6:7], tB[:], axis=AX.X, op=ALU.max, apply_absolute_value=True))
        DR()
        op("vector", lambda e: e.tensor_tensor(
            tiny[:, 7:8], tiny[:, 6:7], scal[:, C_C45:C_C45 + 1], op=ALU.mult))
        DR()
        guard_w("vector", sl_mxa)
        t5d = op("vector", lambda e: e.tensor_tensor(
            mxa[:, 4:5], mxa[:, 4:5], tiny[:, 7:8], op=ALU.max), "dve")
        note_r("dve", t5d, sl_gin[par], sl_upin[par], sl_scal)
        note_w("dve", t5d, sl_tA, sl_tDi, sl_tB, sl_mxa, sl_tiny)
        guard_r("gpsimd", sl_tB, sl_scal)
        guard_w("gpsimd", sl_p4t[par])
        p5d = op("gpsimd", lambda e, par=par: e.tensor_scalar(
            p4t[:, par, :], tB[:], scal[:, C_C45:C_C45 + 1], None, op0=ALU.mult),
            "pool")
        note_r("pool", p5d, sl_tB, sl_scal)
        note_w("pool", p5d, sl_p4t[par])
        wait("scalar", "pool", p5d)
        op("scalar", lambda e, par=par, ic=ic, cs=cs: e.dma_start(
            out=p5_sc[ic, :, cs * 2048:(cs + 1) * 2048], in_=p4t[:, par, :]), "sst", 16)
        c4_st[ch] = st_fl(p5_sc[ic, :, cs * 2048:cs * 2048 + 2])
        note_r("sst", c4_st[ch], sl_p4t[par])

    ar3_done = emit_allreduce(3)

    if debug:
        wait("gpsimd", "sst", max(c3_st))
        op("gpsimd", lambda e: e.dma_start(out=dbg["dbg_p4"], in_=p4_sc), "gst", 16)
        wait("gpsimd", "sst", max(c4_st))
        op("gpsimd", lambda e: e.dma_start(out=dbg["dbg_p5"], in_=p5_sc), "gst", 16)
        wait("gpsimd", "sst", mm_st[(0, NIC * 8 - 1)])
        op("gpsimd", lambda e: e.dma_start(out=dbg["dbg_gate"], in_=gate_sc), "gst", 16)
        wait("gpsimd", "sst", mm_st[(1, NIC * 8 - 1)])
        op("gpsimd", lambda e: e.dma_start(out=dbg["dbg_up"], in_=up_sc), "gst", 16)

    fence()
    r6 = Reg(PC)
    p5i = r6.t("p5i", [128, 2, NIC, 256], F32)
    oin = r6.t("oin", [128, NIC, 256], I32)
    ot = r6.t("ot", [128, 2, NIC, 256], BF16)
    x1c = r6.t("x1c", [128, 2, 2, D], F32)
    outt = r6.t("outt", [128, 2, 2, D], F32)
    sl_p5i, sl_ot, sl_x1c, sl_outt = S(2), S(2), S(2), S(2)
    sl_oin = S()
    seed(ends_now(), sl_p5i, sl_ot, sl_x1c, sl_outt, sl_oin)
    psF = [PS(bk) for bk in range(8)]
    sl_psF = S(2)

    NTG = 16
    c5_st = [0] * NTG
    for tg in range(NTG):
        par = tg % 2
        guard_w("sync", sl_p5i[par], sl_x1c[par])
        wait("sync", "sst", max(c4_st))
        ld_dma(p5i[:, par, :, :],
               p5_sc[:, :, tg * 256:(tg + 1) * 256].rearrange("i p t -> p i t"))
        ld_dma(x1c[:, par, :, :],
               x1_sc[tg * 2:(tg + 1) * 2].rearrange("a p d -> p a d"))
        x1l = pl = ld_fl(p5i[:, par, 0, 0:2])
        note_w("ld", pl, sl_p5i[par], sl_x1c[par])
        wait("vector", "ld", pl)
        if tg == 0:
            wait("vector", "dve", ar3_done)
        guard_r("vector", sl_scal)
        guard_w("vector", sl_oin, sl_ot[par])
        op("vector", lambda e, par=par: e.tensor_scalar(
            oin[:, :, :], p5i[:, par, :, :], scal[:, C_I5:C_I5 + 1], None,
            op0=ALU.mult))
        DR()
        od = op("vector", lambda e, par=par: e.tensor_scalar(
            ot[:, par, :, :], oin[:, :, :], scal[:, C_S5:C_S5 + 1], None,
            op0=ALU.mult), "dve")
        note_r("dve", od, sl_p5i[par], sl_scal)
        note_w("dve", od, sl_oin, sl_ot[par])
        wait("tensor", "dve", od)
        if tg == 0:
            wait("tensor", "act", wd_ready)
        guard_r("tensor", sl_wdT)
        guard_w("tensor", sl_psF[par])
        for tsub in range(2):
            for dch in range(2):
                bank = par * 4 + tsub * 2 + dch
                for icc in range(NIC):
                    op("tensor", lambda e, bank=bank, icc=icc, par=par, tsub=tsub,
                       dch=dch: e.matmul(
                        psF[bank],
                        lhsT=ot[:, par, icc, tsub * 128:(tsub + 1) * 128],
                        rhs=wdT[:, icc, dch * 512:(dch + 1) * 512],
                        start=(icc == 0), stop=(icc == NIC - 1)),
                        "pe" if icc == NIC - 1 else None)
        fm = ctr["pe"]
        note_w("pe", fm, sl_psF[par])
        note_r("pe", fm, sl_ot[par], sl_wdT)
        wait("vector", "pe", fm)
        wait("vector", "ld", x1l)
        guard_w("vector", sl_outt[par])
        for tsub in range(2):
            for dch in range(2):
                bank = par * 4 + tsub * 2 + dch
                op("vector", lambda e, bank=bank, par=par, tsub=tsub, dch=dch:
                   e.tensor_tensor(
                    outt[:, par, tsub, dch * 512:(dch + 1) * 512], psF[bank],
                    x1c[:, par, tsub, dch * 512:(dch + 1) * 512], op=ALU.add),
                    "dve" if (tsub == 1 and dch == 1) else None)
        ed = ctr["dve"]
        note_r("dve", ed, sl_psF[par], sl_x1c[par])
        note_w("dve", ed, sl_outt[par])
        wait("scalar", "dve", ed)
        c5_st[tg] = op("scalar", lambda e, par=par, tg=tg: e.dma_start(
            out=out[tg * 256:(tg + 1) * 256, :].rearrange("(a p) d -> p a d", p=128),
            in_=outt[:, par, :, :]), "sst", 16)
        note_r("sst", c5_st[tg], sl_outt[par])

    wait("sync", "sst", c5_st[NTG - 1])
    if debug:
        wait("gpsimd", "act", wd_ready)
        op("gpsimd", lambda e: e.dma_start(out=dbg["dbg_wdT"], in_=wdT[:]), "gst", 16)
        wait("gpsimd", "dve", ctr["dve"])
        op("gpsimd", lambda e: e.dma_start(out=dbg["dbg_scal"], in_=scal[:]), "gst", 16)
        op("gpsimd", lambda e: e.dma_start(out=dbg["dbg_max"], in_=mxa[:]), "gst", 16)
        wait("gpsimd", "gst", ctr["gst"])



    with ExitStack() as st_ctx:
        for s in SEMS:
            semh[s] = st_ctx.enter_context(nc.semaphore(s))
        block = st_ctx.enter_context(nc.Block())

        @block.sync
        def _(sync):
            for f in prog["sync"]:
                f(sync)

        @block.scalar
        def _(scalar):
            for f in prog["scalar"]:
                f(scalar)

        @block.vector
        def _(vector):
            for f in prog["vector"]:
                f(vector)

        @block.tensor
        def _(tensor):
            for f in prog["tensor"]:
                f(tensor)

        @block.gpsimd
        def _(gpsimd):
            for f in prog["gpsimd"]:
                f(gpsimd)

    ps_cm.__exit__(None, None, None)
    big_cm.__exit__(None, None, None)
    return nc


_NC_CACHE = {}


def _get_nc(debug=False):
    key = bool(debug)
    if key not in _NC_CACHE:
        _NC_CACHE[key] = build(key)
    return _NC_CACHE[key]


def make_in_maps(inputs):
    x = np.asarray(inputs["L0_attn_x"], np.float32)
    b_in = np.asarray(inputs["L0_attn_b_in"], np.float32)
    common = dict(
        w_in=np.ascontiguousarray(np.asarray(inputs["L0_attn_W_in"], np.float32)),
        w_out=np.ascontiguousarray(np.asarray(inputs["L0_attn_W_out"], np.float32)),
        w_gate=np.ascontiguousarray(np.asarray(inputs["L0_ffn_w_gate"], np.float32)),
        w_up=np.ascontiguousarray(np.asarray(inputs["L0_ffn_w_up"], np.float32)),
        w_down=np.ascontiguousarray(np.asarray(inputs["L0_ffn_w_down"], np.float32)),
        ln_g_rep=np.ascontiguousarray(np.broadcast_to(
            np.asarray(inputs["L0_attn_ln_g"], np.float32), (128, D))),
        ln_b_rep=np.ascontiguousarray(np.broadcast_to(
            np.asarray(inputs["L0_attn_ln_b"], np.float32), (128, D))),
        b_in_z=np.ascontiguousarray(b_in[4096:4224, None]),
        b_in_uv=np.ascontiguousarray(b_in[None, 0:4096]),
        b_out_row=np.ascontiguousarray(
            np.asarray(inputs["L0_attn_b_out"], np.float32)[None, :]),
        gq=np.ascontiguousarray(np.asarray(inputs["L0_attn_gamma_q"], np.float32)[:, None]),
        bq=np.ascontiguousarray(np.asarray(inputs["L0_attn_beta_q"], np.float32)[:, None]),
        gk=np.ascontiguousarray(np.asarray(inputs["L0_attn_gamma_k"], np.float32)[:, None]),
        bk=np.ascontiguousarray(np.asarray(inputs["L0_attn_beta_k"], np.float32)[:, None]),
        ones_f=np.ones((1, 128), np.float32),
        eps_i=np.full((128, 1), LN_EPS, np.float32),
        ident_b=np.eye(128, dtype=np.float32).astype(ml_dtypes.bfloat16),
        ident_f=np.eye(128, dtype=np.float32),
    )
    in_maps = []
    for c in range(8):
        b, h = c // 2, c % 2
        m = dict(common)
        m["x_own"] = np.ascontiguousarray(x[b, h * NOWN:(h + 1) * NOWN, :])
        m["x_peer"] = np.ascontiguousarray(x[b, (1 - h) * NOWN:(2 - h) * NOWN, :])
        in_maps.append(m)
    return in_maps


def kernel(**inputs):
    nc = _get_nc(False)
    in_maps = make_in_maps(inputs)
    res = run_bass_kernel_spmd(nc, in_maps, core_ids=list(range(8)))
    outp = np.empty((B, L, D), np.float32)
    for c in range(8):
        b, h = c // 2, c % 2
        outp[b, h * NOWN:(h + 1) * NOWN, :] = res.results[c]["out"]
    return outp


# version 02 (final)
# speedup vs baseline: 1.3751x; 1.3751x over previous
import numpy as np
import ml_dtypes
from contextlib import ExitStack
import concourse.bass as bass
import concourse.mybir as mybir
from concourse.bass_utils import run_bass_kernel_spmd

F32 = mybir.dt.float32
BF16 = mybir.dt.bfloat16
I32 = mybir.dt.int32
AF = mybir.ActivationFunctionType
ALU = mybir.AluOpType
AX = mybir.AxisListType

B, L, D = 4, 8192, 1024
E = 2048
DFF = 3072
NOWN = L // 2
NT = NOWN // 128
NIC = DFF // 128
LN_EPS = 1e-5
QS = 1.0 / 32767.0

ENGS = ("sync", "scalar", "vector", "tensor", "gpsimd")
ESEM = {"sync": (None, "ld"), "scalar": ("act", "sst"), "vector": ("dve", None),
        "tensor": ("pe", None), "gpsimd": ("pool", "gst")}


class Slot:
    __slots__ = ("w", "r")

    def __init__(self):
        self.w = None
        self.r = {}


def build(debug=False):
    nc = bass.Bass("TRN2", target_bir_lowering=False)

    x_own = nc.dram_tensor("x_own", [NOWN, D], F32, kind="ExternalInput").ap()
    x_peer = nc.dram_tensor("x_peer", [NOWN, D], F32, kind="ExternalInput").ap()
    w_in = nc.dram_tensor("w_in", [D, 4224], F32, kind="ExternalInput").ap()
    w_out = nc.dram_tensor("w_out", [E, D], F32, kind="ExternalInput").ap()
    w_gate = nc.dram_tensor("w_gate", [DFF, D], F32, kind="ExternalInput").ap()
    w_up = nc.dram_tensor("w_up", [DFF, D], F32, kind="ExternalInput").ap()
    w_down = nc.dram_tensor("w_down", [D, DFF], F32, kind="ExternalInput").ap()
    ln_g_rep = nc.dram_tensor("ln_g_rep", [128, D], F32, kind="ExternalInput").ap()
    ln_b_rep = nc.dram_tensor("ln_b_rep", [128, D], F32, kind="ExternalInput").ap()
    b_in_z = nc.dram_tensor("b_in_z", [128, 1], F32, kind="ExternalInput").ap()
    b_in_uv = nc.dram_tensor("b_in_uv", [1, 4096], F32, kind="ExternalInput").ap()
    b_out_row = nc.dram_tensor("b_out_row", [1, D], F32, kind="ExternalInput").ap()
    gq_i = nc.dram_tensor("gq", [128, 1], F32, kind="ExternalInput").ap()
    bq_i = nc.dram_tensor("bq", [128, 1], F32, kind="ExternalInput").ap()
    gk_i = nc.dram_tensor("gk", [128, 1], F32, kind="ExternalInput").ap()
    bk_i = nc.dram_tensor("bk", [128, 1], F32, kind="ExternalInput").ap()
    ones_f = nc.dram_tensor("ones_f", [1, 128], F32, kind="ExternalInput").ap()
    eps_i = nc.dram_tensor("eps_i", [128, 1], F32, kind="ExternalInput").ap()
    ident_b_i = nc.dram_tensor("ident_b", [128, 128], BF16, kind="ExternalInput").ap()
    ident_f_i = nc.dram_tensor("ident_f", [128, 128], F32, kind="ExternalInput").ap()
    out = nc.dram_tensor("out", [NOWN, D], F32, kind="ExternalOutput").ap()

    u_sc = nc.dram_tensor("u_sc", [NT, 128, E], BF16).ap()
    v_sc = nc.dram_tensor("v_sc", [2 * NT, 128, E], BF16).ap()
    kT_sc = nc.dram_tensor("kT_sc", [128, L], BF16).ap()
    p1_sc = nc.dram_tensor("p1_sc", [NT, 128, E], F32).ap()
    p12_sc = nc.dram_tensor("p12_sc", [NT, 128, E], F32).ap()
    x1_sc = nc.dram_tensor("x1_sc", [NT, 128, D], F32).ap()
    gate_sc = nc.dram_tensor("gate_sc", [NIC, 128, NOWN], F32).ap()
    up_sc = nc.dram_tensor("up_sc", [NIC, 128, NOWN], F32).ap()
    p4_sc = nc.dram_tensor("p4_sc", [NIC, 128, NOWN], F32).ap()
    p5_sc = nc.dram_tensor("p5_sc", [NIC, 128, NOWN], F32).ap()
    flush_dr = nc.dram_tensor("flush_dr", [128, 16], F32).ap()
    cc_in = [nc.dram_tensor(f"cc{i}_in", [8, 1], F32).ap() for i in range(4)]
    cc_out = [nc.dram_tensor(f"cc{i}_out", [8, 1], F32).ap() for i in range(4)]

    dbg = {}
    if debug:
        for nm, shp, dt in [
            ("dbg_kT", [128, L], BF16), ("dbg_qT", [128, NOWN], BF16),
            ("dbg_u0", [128, E], BF16), ("dbg_v0", [128, E], BF16),
            ("dbg_p12", [NT, 128, E], F32), ("dbg_x1", [NT, 128, D], F32),
            ("dbg_gate", [NIC, 128, NOWN], F32), ("dbg_up", [NIC, 128, NOWN], F32),
            ("dbg_p4", [NIC, 128, NOWN], F32), ("dbg_p5", [NIC, 128, NOWN], F32),
            ("dbg_scal", [128, 64], F32), ("dbg_max", [128, 8], F32),
            ("dbg_xg", [128, D], F32), ("dbg_xln", [128, D], F32),
            ("dbg_xnb", [128, D], BF16), ("dbg_zs", [128, 512], F32),
            ("dbg_mrl", [128, 8], F32), ("dbg_stt", [128, 16], F32),
            ("dbg_wdT", [128, NIC, D], BF16),
            ("dbg_wci", [128, D], F32), ("dbg_wqb", [128, D], BF16),
            ("dbg_wsnap", [128, D], F32),
            ("dbg_wTc", [128, 8, 128], BF16), ("dbg_xqT", [128, 8, 512], BF16),
        ]:
            dbg[nm] = nc.dram_tensor(nm, shp, dt, kind="ExternalOutput").ap()

    prog = {e: [] for e in ENGS}
    SEMS = ["ld", "sst", "gst", "pe", "act", "dve", "pool",
            "cc0", "cc1", "cc2", "cc3"]
    ctr = {s: 0 for s in SEMS}
    semh = {}

    def op(eng, f, inc=None, k=1):
        if inc is None:
            prog[eng].append(f)
            return None
        prog[eng].append(lambda e, f=f, s=inc, k=k: f(e).then_inc(semh[s], k))
        ctr[inc] += k
        return ctr[inc]

    def wait(eng, s, v):
        if v and v > 0:
            prog[eng].append(lambda e, s=s, v=int(v): e.wait_ge(semh[s], int(v)))

    def DR(eng="vector"):
        prog[eng].append(lambda e: e.drain())

    OWN = {"sync": ("ld",), "scalar": ("act",), "vector": ("dve",),
           "tensor": ("pe",), "gpsimd": ("pool",)}

    def guard_w(eng, *slots):
        for sl in slots:
            for s, c in sl.r.items():
                if s not in OWN[eng]:
                    wait(eng, s, c)
            if sl.w is not None and sl.w[0] not in OWN[eng]:
                wait(eng, sl.w[0], sl.w[1])

    def guard_r(eng, *slots):
        for sl in slots:
            if sl.w is not None and sl.w[0] not in OWN[eng]:
                wait(eng, sl.w[0], sl.w[1])

    def note_w(sem, cnt, *slots):
        for sl in slots:
            sl.w = (sem, cnt)
            sl.r = {}

    def note_r(sem, cnt, *slots):
        for sl in slots:
            sl.r[sem] = max(sl.r.get(sem, 0), cnt)

    def S(n=1):
        return [Slot() for _ in range(n)] if n > 1 else Slot()

    def seed(ends, *slots):
        for sl in slots:
            for lst in (sl if isinstance(sl, list) else [sl]):
                lst.r = dict(ends)
                lst.w = None

    def ends_now():
        return {s: ctr[s] for s in ("pe", "act", "dve", "pool", "sst", "ld", "gst")
                if ctr[s] > 0}

    def fence():
        for eng, s in (("scalar", "act"), ("vector", "dve"),
                       ("tensor", "pe"), ("gpsimd", "pool")):
            op(eng, lambda e: e.drain(), s)

    SB_LIMIT = 207 * 1024
    big_cm = nc.sbuf_tensor("bigsb", [128, SB_LIMIT], mybir.dt.uint8)
    big_h = big_cm.__enter__()
    big_addr = nc.lookup_mloc(big_h).addr
    _uid = [0]

    def mk_sb(name, shape, dtype, off):
        _uid[0] += 1
        h = nc.virtual_sbuf_tensor(f"{name}_{_uid[0]}", shape, dtype)
        nc._tensor(h.name, list(h.shape), h.dtype, type="SB")
        mloc = nc.lookup_mloc(h)
        mloc.allocated = True
        mloc.addr = big_addr + off
        return h

    class Reg:
        def __init__(self, base):
            self.off = int(base)

        def t(self, name, shape, dtype):
            nbytes = int(np.prod(shape[1:])) * mybir.dt.size(dtype)
            o = self.off
            self.off = (o + nbytes + 31) & ~31
            assert self.off <= SB_LIMIT, f"sbuf overflow at {name}: {self.off}"
            return mk_sb(name, shape, dtype, o)

    ps_cm = nc.psum_tensor("psbig", [128, 4096], F32)
    psB = ps_cm.__enter__()

    def PS(bank, nbanks=1, dt=F32):
        v = psB[:, bank * 512:(bank + nbanks) * 512]
        return v if dt == F32 else v.bitcast(dt)

    per = Reg(0)
    qT = per.t("qT", [128, NOWN], BF16)
    onest = per.t("onest", [1, 128], F32)
    identb = per.t("identb", [128, 128], BF16)
    identf = per.t("identf", [128, 128], F32)
    bz = per.t("bz", [128, 1], F32)
    gqs = per.t("gqs", [128, 4], F32)
    epsc = per.t("epsc", [128, 1], F32)
    st3 = per.t("st3", [128, NT, 2], F32)
    mxa = per.t("mxa", [128, 8], F32)
    scal = per.t("scal", [128, 64], F32)
    tiny = per.t("tiny", [128, 32], F32)
    stt = per.t("stt", [128, 16], F32)
    flsb = per.t("flsb", [128, 16], F32)
    PB = per.off
    sl_qT, sl_st3, sl_mxa, sl_scal, sl_tiny = S(), S(), S(), S(), S()
    sl_stt = S(2)

    C_S1, C_I1 = 0 + 0, 8 + 0
    C_S3, C_I3 = 16 + 2, 24 + 2
    C_S2, C_I2 = 32 + 1, 40 + 1
    C_S4, C_I4 = 32 + 3, 40 + 3
    C_S5, C_I5 = 48 + 4, 56 + 4
    C_C4 = 48 + 0
    C_C45 = 48 + 1

    def ld_dma(dst, src):
        return op("sync", lambda e, d=dst, s=src: e.dma_start(out=d, in_=s), "ld", 16)

    def _fl_norm(b):
        if b.dtype != F32:
            b = b.bitcast(F32)
        nfree = int(np.prod(b.shape[1:]))
        P = int(b.shape[0])
        return b, P, min(nfree, 2)

    def ld_fl(buf2col):
        b, P, n = _fl_norm(buf2col)
        return op("sync", lambda e, b=b, P=P, n=n: e.dma_start(
            out=flush_dr[0:P, 0:n], in_=b[:, 0:n] if b.ndim == 2 else b),
            "ld", 16)

    def st_fl(dram2col, eng="scalar"):
        sem = "sst" if eng == "scalar" else "gst"
        b, P, n = _fl_norm(dram2col)
        return op(eng, lambda e, b=b, P=P, n=n: e.dma_start(
            out=flsb[0:P, 0:n], in_=b[:, 0:n] if b.ndim == 2 else b), sem, 16)

    for dst, src in [(onest[:], ones_f), (identb[:], ident_b_i), (identf[:], ident_f_i),
                     (bz[:], b_in_z), (gqs[:, 0:1], gq_i), (gqs[:, 1:2], bq_i),
                     (gqs[:, 2:3], gk_i), (gqs[:, 3:4], bk_i),
                     (epsc[:], eps_i)]:
        ld_dma(dst, src)
    ld_const = ld_fl(gqs[:, 0:2])
    wait("vector", "ld", ld_const)
    mz = op("vector", lambda e: e.memset(mxa[:], 0.0), "dve")
    note_w("dve", mz, sl_mxa)

    def ln_stats(par, m_dst, r_dst, m_sl, r_sl):
        b0 = par * 8
        DR()
        op("vector", lambda e: e.tensor_scalar(
            stt[:, b0 + 5:b0 + 6], stt[:, b0 + 4:b0 + 5], 1.0 / D, None, op0=ALU.mult))
        DR()
        op("vector", lambda e: e.tensor_tensor(
            stt[:, b0 + 6:b0 + 7], stt[:, b0 + 5:b0 + 6], stt[:, b0 + 5:b0 + 6],
            op=ALU.mult))
        DR()
        op("vector", lambda e: e.scalar_tensor_tensor(
            stt[:, b0 + 7:b0 + 8], stt[:, b0 + 0:b0 + 1], 1.0 / D, stt[:, b0 + 6:b0 + 7],
            op0=ALU.mult, op1=ALU.subtract), "dve")
        v1 = ctr["dve"]
        wait("scalar", "dve", v1)
        op("scalar", lambda e: e.activation(
            stt[:, b0 + 6:b0 + 7], stt[:, b0 + 7:b0 + 8], AF.Sqrt,
            bias=epsc[:], scale=1.0), "act")
        va = ctr["act"]
        wait("vector", "act", va)
        if m_dst is not None:
            op("vector", lambda e: e.tensor_copy(m_dst, stt[:, b0 + 5:b0 + 6]))
        op("vector", lambda e: e.reciprocal(r_dst, stt[:, b0 + 6:b0 + 7]), "dve")
        DR()
        v2 = ctr["dve"]
        note_r("dve", v2, sl_stt[par])
        note_w("dve", v2, m_sl, r_sl)
        return v2

    mloc = per_m = None

    ra = Reg(PB)
    Wbf = ra.t("Wbf", [128, 8, 4224], BF16)
    lgr = ra.t("lgr", [128, D], F32)
    lbr = ra.t("lbr", [128, D], F32)
    buvr = ra.t("buvr", [1, 4096], F32)
    xnT = ra.t("xnT", [128, 2, 8, 512], BF16)
    ut = ra.t("ut", [128, 2, E], BF16)
    vt = ra.t("vt", [128, 2, E], BF16)
    xg = ra.t("xg", [128, 2, 4, D], F32)
    mrl = ra.t("mrl", [128, 8], F32)
    ra_sub = ra.off
    wst = ra.t("wst", [128, 2, 4224], F32)

    sl_Wbf, sl_lgr, sl_lbr, sl_buvr = S(), S(), S(), S()
    sl_xnT, sl_ut, sl_vt, sl_xg = S(2), S(2), S(2), S(2)
    sl_wst = S(2)
    sl_mrl = S(2)

    psA = [PS(0, 1, BF16), PS(1, 1, BF16)]
    psZ = [PS(2), PS(3)]
    psUV = [PS(4), PS(5)]

    ld_dma(lgr[:], ln_g_rep)
    ld_dma(lbr[:], ln_b_rep)
    ld_dma(buvr[:], b_in_uv)
    lf0 = ld_fl(lgr[:, 0:2])
    note_w("ld", lf0, sl_lgr, sl_lbr, sl_buvr)

    for dc in range(8):
        guard_w("sync", sl_wst[dc % 2])
        ld_dma(wst[:, dc % 2, :], w_in[dc * 128:(dc + 1) * 128, :])
        lw = ld_fl(wst[:, dc % 2, 0:2])
        note_w("ld", lw, sl_wst[dc % 2])
        wait("vector", "ld", lw)
        wc = op("vector", lambda e, dc=dc: e.tensor_copy(
            Wbf[:, dc, :], wst[:, dc % 2, :]), "dve")
        note_r("dve", wc, sl_wst[dc % 2])
    note_w("dve", ctr["dve"], sl_Wbf)
    w_cast_done = ctr["dve"]
    rt_a = Reg(ra_sub)
    xlnt = rt_a.t("xlnt", [128, 2, D], F32)
    xnf = rt_a.t("xnf", [128, 2, D], F32)
    xnb = rt_a.t("xnb", [128, 2, D], BF16)
    zs = rt_a.t("zs", [128, 2, 512], F32)
    ktmp = rt_a.t("ktmp", [128, 2, 512], BF16)
    junkA = rt_a.t("junkA", [128, 2, D], F32)
    assert rt_a.off <= SB_LIMIT
    sl_xlnt, sl_xnf, sl_xnb, sl_zs, sl_ktmp = S(2), S(2), S(2), S(2), S(2)
    sl_junkA = S(2)
    seed({"dve": w_cast_done}, sl_xlnt, sl_xnf, sl_xnb, sl_zs, sl_ktmp, sl_junkA)

    xld = [0] * 16
    st_u = [0] * NT
    st_v = [0] * (2 * NT)
    st_kt = [0] * 16
    sl_psA = S(2)
    sl_psZ = S(2)
    sl_psUV = S(2)

    def a_xload(gi):
        src = x_own if gi < 8 else x_peer
        g = gi % 8
        guard_w("sync", sl_xg[gi % 2])
        ld_dma(xg[:, gi % 2, :, :],
               src[g * 512:(g + 1) * 512, :].rearrange("(a p) d -> p a d", p=128))
        xld[gi] = ld_fl(xg[:, gi % 2, 0, 0:2])
        note_w("ld", xld[gi], sl_xg[gi % 2])

    a_xload(0)
    a_xload(1)

    for gi in range(16):
        own = gi < 8
        g = gi % 8
        par = gi % 2
        for a in range(4):
            t_abs = gi * 4 + a
            sp = t_abs % 2
            b0 = sp * 8
            xa = xg[:, par, a, :]
            guard_r("scalar", sl_xg[par])
            guard_w("scalar", sl_junkA[sp])
            sq1 = op("scalar", lambda e, xa=xa, sp=sp: e.activation(
                junkA[:, sp, :], xa, AF.Square), "act")
            note_w("act", sq1, sl_junkA[sp])
            note_r("act", sq1, sl_xg[par])
            guard_r("vector", sl_xg[par])
            guard_w("vector", sl_stt[sp])
            rs1 = op("vector", lambda e, xa=xa, b0=b0: e.tensor_reduce(
                stt[:, b0 + 4:b0 + 5], xa, axis=AX.X, op=ALU.add), "dve")
            note_r("dve", rs1, sl_xg[par])
            wait("vector", "act", sq1)
            rq1 = op("vector", lambda e, sp=sp, b0=b0: e.tensor_reduce(
                stt[:, b0 + 0:b0 + 1], junkA[:, sp, :], axis=AX.X, op=ALU.add), "dve")
            note_r("dve", rq1, sl_junkA[sp])
            lsd = ln_stats(sp, mrl[:, par * 4:par * 4 + 1], mrl[:, par * 4 + 1:par * 4 + 2],
                           sl_mrl[par], sl_mrl[par])
            if debug and gi == 0 and a == 0:
                wait("gpsimd", "dve", ctr["dve"])
                dm = op("gpsimd", lambda e: e.dma_start(
                    out=dbg["dbg_mrl"], in_=mrl[:]), "gst", 16)
                note_r("gst", dm, sl_mrl[0])
                ds = op("gpsimd", lambda e: e.dma_start(
                    out=dbg["dbg_stt"], in_=stt[:]), "gst", 16)
                note_r("gst", ds, sl_stt[0])
            guard_w("vector", sl_xlnt[par])
            xlv = op("vector", lambda e, xa=xa, par=par: e.tensor_scalar(
                xlnt[:, par, :], xa, mrl[:, par * 4:par * 4 + 1],
                mrl[:, par * 4 + 1:par * 4 + 2], op0=ALU.subtract, op1=ALU.mult), "dve")
            note_r("dve", xlv, sl_xg[par], sl_mrl[par])
            note_w("dve", xlv, sl_xlnt[par])
            guard_r("scalar", sl_xlnt[par])
            guard_w("scalar", sl_junkA[sp])
            sq2 = op("scalar", lambda e, par=par, sp=sp: e.activation(
                junkA[:, sp, :], xlnt[:, par, :], AF.Square), "act")
            note_w("act", sq2, sl_junkA[sp])
            note_r("act", sq2, sl_xlnt[par])
            guard_w("vector", sl_stt[sp])
            rs2 = op("vector", lambda e, par=par, b0=b0: e.tensor_reduce(
                stt[:, b0 + 4:b0 + 5], xlnt[:, par, :], axis=AX.X, op=ALU.add), "dve")
            note_r("dve", rs2, sl_xlnt[par])
            wait("vector", "act", sq2)
            rq2 = op("vector", lambda e, sp=sp, b0=b0: e.tensor_reduce(
                stt[:, b0 + 0:b0 + 1], junkA[:, sp, :], axis=AX.X, op=ALU.add), "dve")
            note_r("dve", rq2, sl_junkA[sp])
            ln_stats(sp, mrl[:, par * 4 + 2:par * 4 + 3], mrl[:, par * 4 + 3:par * 4 + 4],
                     sl_mrl[par], sl_mrl[par])
            guard_w("vector", sl_xnf[par])
            xnv = op("vector", lambda e, par=par: e.tensor_scalar(
                xnf[:, par, :], xlnt[:, par, :], mrl[:, par * 4 + 2:par * 4 + 3],
                mrl[:, par * 4 + 3:par * 4 + 4], op0=ALU.subtract, op1=ALU.mult), "dve")
            note_r("dve", xnv, sl_xlnt[par], sl_mrl[par])
            note_w("dve", xnv, sl_xnf[par])
            if debug and gi == 0 and a == 0:
                wait("gpsimd", "ld", xld[0])
                d1 = op("gpsimd", lambda e: e.dma_start(
                    out=dbg["dbg_xg"], in_=xg[:, 0, 0, :]), "gst", 16)
                note_r("gst", d1, sl_xg[0])
                wait("gpsimd", "dve", xlv)
                d2 = op("gpsimd", lambda e: e.dma_start(
                    out=dbg["dbg_xln"], in_=xlnt[:, 0, :]), "gst", 16)
                note_r("gst", d2, sl_xlnt[0])
            guard_r("vector", sl_lgr, sl_lbr)
            DR()
            op("vector", lambda e, par=par: e.tensor_tensor(
                xnf[:, par, :], xnf[:, par, :], lgr[:], op=ALU.mult))
            DR()
            guard_w("vector", sl_xnb[par])
            xnbd = op("vector", lambda e, par=par: e.tensor_tensor(
                xnb[:, par, :], xnf[:, par, :], lbr[:], op=ALU.add), "dve")
            note_w("dve", xnbd, sl_xnb[par], sl_xnf[par])
            note_r("dve", xnbd, sl_lgr, sl_lbr)
            DR()
            if debug and gi == 0 and a == 0:
                d3 = op("gpsimd", lambda e: e.dma_start(
                    out=dbg["dbg_xnb"], in_=xnb[:, 0, :]), "gst", 16)
                note_r("gst", d3, sl_xnb[0])
            par2 = t_abs % 2
            guard_r("tensor", sl_xnb[par])
            guard_w("tensor", sl_psA[par2])
            for dc in range(8):
                op("tensor", lambda e, par=par, par2=par2, dc=dc: e.transpose(
                    psA[par2][:, dc * 128:(dc + 1) * 128],
                    xnb[:, par, dc * 128:(dc + 1) * 128], identb[:]))
            op("tensor", lambda e: e.drain(), "pe")
            trd = ctr["pe"]
            note_r("pe", trd, sl_xnb[par])
            note_w("pe", trd, sl_psA[par2])
            wait("scalar", "pe", trd)
            guard_w("scalar", sl_xnT[par])
            ev = op("scalar", lambda e, par=par, par2=par2, a=a: e.activation(
                xnT[:, par, :, a * 128:(a + 1) * 128],
                psA[par2].rearrange("p (dc c) -> p dc c", c=128),
                AF.Identity), "act")
            note_r("act", ev, sl_psA[par2])
            note_w("act", ev, sl_xnT[par])
            prog["scalar"].append(lambda e: e.drain())
        xnT_ready = ctr["act"]
        if gi + 2 < 16:
            a_xload(gi + 2)

        wait("tensor", "dve", w_cast_done)
        wait("tensor", "act", xnT_ready)
        guard_w("tensor", sl_psZ[par])
        for dc in range(8):
            op("tensor", lambda e, par=par, dc=dc: e.matmul(
                psZ[par], lhsT=Wbf[:, dc, 4096:4224], rhs=xnT[:, par, dc, :],
                start=(dc == 0), stop=(dc == 7)), "pe" if dc == 7 else None)
        zmm = ctr["pe"]
        note_w("pe", zmm, sl_psZ[par])
        note_r("pe", zmm, sl_xnT[par], sl_Wbf)
        wait("scalar", "pe", zmm)
        wait("scalar", "ld", ld_const)
        guard_w("scalar", sl_zs[par])
        zev = op("scalar", lambda e, par=par: e.activation(
            zs[:, par, :], psZ[par], AF.Silu, bias=bz[:]), "act")
        note_r("act", zev, sl_psZ[par])
        note_w("act", zev, sl_zs[par])
        if debug and gi == 0:
            wait("gpsimd", "act", zev)
            d4 = op("gpsimd", lambda e: e.dma_start(
                out=dbg["dbg_zs"], in_=zs[:, 0, :]), "gst", 16)
            note_r("gst", d4, sl_zs[0])
        wait("vector", "act", zev)
        if own:
            qv = op("vector", lambda e, par=par, g=g: e.tensor_scalar(
                qT[:, g * 512:(g + 1) * 512], zs[:, par, :], gqs[:, 0:1], gqs[:, 1:2],
                op0=ALU.mult, op1=ALU.add), "dve")
            note_r("dve", qv, sl_zs[par])
            note_w("dve", qv, sl_qT)
        guard_w("vector", sl_ktmp[par])
        kv = op("vector", lambda e, par=par: e.tensor_scalar(
            ktmp[:, par, :], zs[:, par, :], gqs[:, 2:3], gqs[:, 3:4],
            op0=ALU.mult, op1=ALU.add), "dve")
        note_r("dve", kv, sl_zs[par])
        note_w("dve", kv, sl_ktmp[par])
        wait("scalar", "dve", kv)
        st_kt[gi] = op("scalar", lambda e, par=par, gi=gi: e.dma_start(
            out=kT_sc[:, gi * 512:(gi + 1) * 512], in_=ktmp[:, par, :]), "sst", 16)
        note_r("sst", st_kt[gi], sl_ktmp[par])

        for a in range(4):
            t_abs = gi * 4 + a
            for which in (["u", "v"] if own else ["v"]):
                col0 = 0 if which == "u" else E
                dstb = ut if which == "u" else vt
                sl_dst = sl_ut if which == "u" else sl_vt
                tp = t_abs % 2
                for ec in range(4):
                    pp = (t_abs * 4 + ec) % 2
                    guard_w("tensor", sl_psUV[pp])
                    guard_r("tensor", sl_buvr)
                    op("tensor", lambda e, pp=pp, col0=col0, ec=ec: e.matmul(
                        psUV[pp], lhsT=onest[:],
                        rhs=buvr[:, col0 + ec * 512: col0 + (ec + 1) * 512],
                        start=True, stop=False))
                    for dc in range(8):
                        op("tensor", lambda e, pp=pp, par=par, dc=dc, a=a, col0=col0,
                           ec=ec: e.matmul(
                            psUV[pp], lhsT=xnT[:, par, dc, a * 128:(a + 1) * 128],
                            rhs=Wbf[:, dc, col0 + ec * 512: col0 + (ec + 1) * 512],
                            start=False, stop=(dc == 7)), "pe" if dc == 7 else None)
                    um = ctr["pe"]
                    note_w("pe", um, sl_psUV[pp])
                    note_r("pe", um, sl_xnT[par], sl_Wbf, sl_buvr)
                    wait("scalar", "pe", um)
                    if ec == 0:
                        guard_w("scalar", sl_dst[tp])
                    sev = op("scalar", lambda e, pp=pp, tp=tp, dstb=dstb, ec=ec:
                             e.activation(dstb[:, tp, ec * 512:(ec + 1) * 512],
                                          psUV[pp], AF.Silu), "act")
                    note_r("act", sev, sl_psUV[pp])
                note_w("act", ctr["act"], sl_dst[tp])
                wait("scalar", "act", ctr["act"])
                if which == "u":
                    st_u[t_abs] = op("scalar", lambda e, tp=tp, t=t_abs: e.dma_start(
                        out=u_sc[t], in_=ut[:, tp, :]), "sst", 16)
                    note_r("sst", st_u[t_abs], sl_dst[tp])
                else:
                    st_v[t_abs] = op("scalar", lambda e, tp=tp, t=t_abs: e.dma_start(
                        out=v_sc[t], in_=vt[:, tp, :]), "sst", 16)
                    note_r("sst", st_v[t_abs], sl_dst[tp])

    a_st_flush = st_fl(v_sc[0, :, 0:4])
    fence()
    a_end = {s: ctr[s] for s in ("pe", "act", "dve", "pool")}

    if debug:
        wait("gpsimd", "sst", max(st_kt))
        op("gpsimd", lambda e: e.dma_start(out=dbg["dbg_kT"], in_=kT_sc), "gst", 16)
        wait("gpsimd", "dve", a_end["dve"])
        op("gpsimd", lambda e: e.dma_start(out=dbg["dbg_qT"], in_=qT[:]), "gst", 16)
        wait("gpsimd", "sst", max(st_u))
        op("gpsimd", lambda e: e.dma_start(out=dbg["dbg_u0"], in_=u_sc[0]), "gst", 16)
        wait("gpsimd", "sst", max(st_v))
        op("gpsimd", lambda e: e.dma_start(out=dbg["dbg_v0"], in_=v_sc[0]), "gst", 16)

    rb = Reg(PB)
    v_sb = rb.t("v_sb", [128, 32, E], BF16)
    s2T = rb.t("s2T", [128, 2, 32, 256], BF16)
    kTh = rb.t("kTh", [128, NOWN], BF16)
    rt = rb.t("rt", [128, 2, 256], F32)
    evb = rb.t("evb", [128, 2, 512], F32)
    p1in = rb.t("p1in", [128, 2, 4, 512], F32)
    sl_vsb, sl_kTh = S(), S()
    sl_s2T, sl_rt, sl_evb = S(2), S(2), S(2)
    sl_p1in = S(2)
    seed(ends_now(), sl_vsb, sl_kTh, sl_s2T, sl_rt, sl_evb, sl_p1in)

    psO = [PS(ec) for ec in range(4)]
    psS = [PS(4), PS(5)]
    sl_psS = S(2)
    sl_psO = S()

    NQG = 16
    b_end = {}
    b1_last_st = 0

    for half in range(2):
        is2 = half == 1
        wait("sync", "sst", a_st_flush)
        for s in ("pe", "act", "dve", "pool"):
            wait("sync", s, a_end[s])
        if is2:
            for s in ("pe", "act", "dve"):
                wait("sync", s, b_end[s])
            wait("sync", "sst", b1_last_st)
        guard_w("sync", sl_vsb, sl_kTh)
        for q in range(4):
            ld_dma(v_sb[:, q * 8:(q + 1) * 8, :],
                   v_sc[half * 32 + q * 8: half * 32 + (q + 1) * 8].rearrange(
                       "a p e -> p a e"))
        ld_dma(kTh[:], kTh_src := kT_sc[:, half * NOWN:(half + 1) * NOWN])
        kld = ld_fl(v_sb[:, 0, 0:4])
        note_w("ld", kld, sl_vsb, sl_kTh)
        vld_all = kld

        sq_dve = {}
        beta_pe = {}
        evac_done = {}
        stq_ec = {}
        p1ld = {}

        def emit_alpha(qg, half=half, is2=is2, vld_all=vld_all, kld=kld,
                       sq_dve=None, beta_pe=None):
            pass

        def emit_alpha(qg):
            guard_r("tensor", sl_kTh, sl_qT)
            guard_w("tensor", sl_s2T[qg % 2])
            for kt in range(32):
                sp = kt % 2
                guard_w("tensor", sl_psS[sp])
                mm = op("tensor", lambda e, sp=sp, kt=kt, qg=qg: e.matmul(
                    psS[sp][:, 0:256], lhsT=kTh[:, kt * 128:(kt + 1) * 128],
                    rhs=qT[:, qg * 256:(qg + 1) * 256], start=True, stop=True), "pe")
                note_w("pe", mm, sl_psS[sp])
                if kt == 31:
                    note_r("pe", mm, sl_kTh, sl_qT)
                wait("scalar", "pe", mm)
                guard_w("scalar", sl_rt[sp])
                rl = op("scalar", lambda e, sp=sp: e.activation(
                    rt[:, sp, :], psS[sp][:, 0:256], AF.Relu, scale=2.0 ** -13), "act")
                note_r("act", rl, sl_psS[sp])
                note_w("act", rl, sl_rt[sp])
                wait("vector", "act", rl)
                sq = op("vector", lambda e, sp=sp, kt=kt, qg=qg: e.tensor_tensor(
                    s2T[:, qg % 2, kt, :], rt[:, sp, :], rt[:, sp, :], op=ALU.mult),
                    "dve")
                note_r("dve", sq, sl_rt[sp])
                sq_dve[(qg, kt)] = sq
            note_w("dve", sq_dve[(qg, 31)], sl_s2T[qg % 2])

        def emit_beta(qg, qsub):
            wait("tensor", "dve", sq_dve[(qg, 31)])
            wait("tensor", "ld", vld_all)
            guard_w("tensor", sl_psO)
            for kt in range(32):
                for ec in range(4):
                    op("tensor", lambda e, qg=qg, qsub=qsub, kt=kt, ec=ec: e.matmul(
                        psO[ec],
                        lhsT=s2T[:, qg % 2, kt, qsub * 128:(qsub + 1) * 128],
                        rhs=v_sb[:, kt, ec * 512:(ec + 1) * 512],
                        start=(kt == 0), stop=(kt == 31)),
                        "pe" if (kt == 31 and ec == 3) else None)
            bp = ctr["pe"]
            beta_pe[(qg, qsub)] = bp
            note_w("pe", bp, sl_psO)
            note_r("pe", bp, sl_s2T[qg % 2], sl_vsb)

        def emit_p1load(qg, qsub):
            qt = qg * 2 + qsub
            guard_w("sync", sl_p1in[qsub])
            wait("sync", "sst", b1_stq[qt])
            ld_dma(p1in[:, qsub], p1_sc[qt].rearrange("p (a c) -> p a c", c=512))
            p1ld[(qg, qsub)] = ld_fl(p1in[:, qsub, 0, 0:2])
            note_w("ld", p1ld[(qg, qsub)], sl_p1in[qsub])

        def emit_evac(qg, qsub):
            qt = qg * 2 + qsub
            dst_sc = p1_sc if not is2 else p12_sc
            eng = "scalar" if not is2 else "vector"
            sem = "act" if not is2 else "dve"
            wait(eng, "pe", beta_pe[(qg, qsub)])
            if is2:
                wait(eng, "ld", p1ld[(qg, qsub)])
            for ec in range(4):
                ep = ec % 2
                guard_w(eng, sl_evb[ep])
                if not is2:
                    ee = op(eng, lambda e, ec=ec, ep=ep: e.activation(
                        evb[:, ep, :], psO[ec], AF.Identity), sem)
                else:
                    ee = op(eng, lambda e, ec=ec, ep=ep, qsub=qsub: e.tensor_tensor(
                        evb[:, ep, :], psO[ec], p1in[:, qsub, ec, :], op=ALU.add), sem)
                    note_r(sem, ee, sl_p1in[qsub])
                note_w(sem, ee, sl_evb[ep])
                wait("scalar", sem, ee)
                stc = op("scalar", lambda e, qt=qt, ec=ec, ep=ep, dst_sc=dst_sc:
                         e.dma_start(out=dst_sc[qt, :, ec * 512:(ec + 1) * 512],
                                     in_=evb[:, ep, :]), "sst", 16)
                note_r("sst", stc, sl_evb[ep])
                stq_ec[(qt, ec)] = stc
            note_r(sem, ctr[sem], sl_psO)
            evac_done[(qg, qsub)] = ctr[sem]
            cur_stq[qt] = st_fl(dst_sc[qt, :, 0:2])

        cur_stq = [0] * NT
        if is2:
            emit_p1load(0, 0)
            emit_p1load(0, 1)
        emit_alpha(0)
        for qg in range(NQG):
            emit_beta(qg, 0)
            if qg + 1 < NQG:
                emit_alpha(qg + 1)
            emit_evac(qg, 0)
            emit_beta(qg, 1)
            emit_evac(qg, 1)
            if is2 and qg + 1 < NQG:
                emit_p1load(qg + 1, 0)
                emit_p1load(qg + 1, 1)
        if not is2:
            fence()
            b_end = {s: ctr[s] for s in ("pe", "act", "dve")}
            b1_last_st = cur_stq[NT - 1]
            b1_stq = list(cur_stq)
    fence()
    b2_end = {s: ctr[s] for s in ("pe", "act", "dve")}
    b2_last_st = cur_stq[NT - 1]
    b2_stq = list(cur_stq)

    if debug:
        wait("gpsimd", "sst", b2_last_st)
        op("gpsimd", lambda e: e.dma_start(out=dbg["dbg_p12"], in_=p12_sc), "gst", 16)

    r3 = Reg(PB)
    Wob = r3.t("Wob", [128, 16, D], BF16)
    wost = r3.t("wost", [128, 2, D], F32)
    bor = r3.t("bor", [1, D], F32)
    p12i = r3.t("p12i", [128, 2, E], F32)
    ui = r3.t("ui", [128, 2, E], BF16)
    xi = r3.t("xi", [128, 2, D], F32)
    xl3 = r3.t("xl3", [128, D], F32)
    o2 = r3.t("o2", [128, 2, E], BF16)
    o2T = r3.t("o2T", [128, 2, 16, 128], BF16)
    xpx = r3.t("xpx", [128, 2, D], F32)
    x1t = r3.t("x1t", [128, 2, D], F32)
    tmp3 = r3.t("tmp3", [128, D], F32)
    junk3 = r3.t("junk3", [128, 2, D], F32)
    sl_Wob, sl_bor, sl_xl3, sl_tmp3 = S(), S(), S(), S()
    sl_junk3 = S(2)
    sl_wost, sl_p12i, sl_ui, sl_xi = S(2), S(2), S(2), S(2)
    sl_o2, sl_o2T, sl_xpx, sl_x1t = S(2), S(2), S(2), S(2)
    seed(ends_now(), sl_Wob, sl_bor, sl_xl3, sl_tmp3, sl_junk3, sl_wost,
         sl_p12i, sl_ui, sl_xi, sl_o2, sl_o2T, sl_xpx, sl_x1t)

    psT3 = [PS(0, 2, BF16), PS(2, 2, BF16)]
    psP3 = [PS(4), PS(5)]
    sl_psT3, sl_psP3 = S(2), S(2)

    for s in ("pe", "act", "dve"):
        wait("sync", s, b2_end[s])
    wait("sync", "sst", b2_last_st)
    bl = ld_dma(bor[:], b_out_row)
    note_w("ld", bl, sl_bor)
    for ec in range(16):
        guard_w("sync", sl_wost[ec % 2])
        ld_dma(wost[:, ec % 2, :], w_out[ec * 128:(ec + 1) * 128, :])
        lw = ld_fl(wost[:, ec % 2, 0:2])
        note_w("ld", lw, sl_wost[ec % 2])
        wait("vector", "ld", lw)
        wc = op("vector", lambda e, ec=ec: e.tensor_copy(
            Wob[:, ec, :], wost[:, ec % 2, :]), "dve")
        note_r("dve", wc, sl_wost[ec % 2])
    note_w("dve", ctr["dve"], sl_Wob)

    b3ld = [0] * NT
    b3_x1st = [0] * NT

    def b3_load(t):
        par = t % 2
        guard_w("sync", sl_p12i[par], sl_ui[par], sl_xi[par])
        wait("sync", "sst", b2_stq[t])
        ld_dma(p12i[:, par, :], p12_sc[t])
        ld_dma(ui[:, par, :], u_sc[t])
        ld_dma(xi[:, par, :], x_own[t * 128:(t + 1) * 128, :])
        b3ld[t] = ld_fl(p12i[:, par, 0:2])
        note_w("ld", b3ld[t], sl_p12i[par], sl_ui[par], sl_xi[par])

    b3_load(0)
    b3_load(1)
    for t in range(NT):
        par = t % 2
        sp = t % 2
        b0 = sp * 8
        guard_r("vector", sl_p12i[par], sl_ui[par])
        guard_w("vector", sl_o2[par])
        o2d = op("vector", lambda e, par=par: e.tensor_tensor(
            o2[:, par, :], p12i[:, par, :], ui[:, par, :], op=ALU.mult), "dve")
        note_r("dve", o2d, sl_p12i[par], sl_ui[par])
        note_w("dve", o2d, sl_o2[par])
        DR()
        guard_r("scalar", sl_xi[par])
        guard_w("scalar", sl_junk3[sp])
        sq1 = op("scalar", lambda e, par=par, sp=sp: e.activation(
            junk3[:, sp, :], xi[:, par, :], AF.Square), "act")
        note_w("act", sq1, sl_junk3[sp])
        note_r("act", sq1, sl_xi[par])
        guard_r("vector", sl_xi[par])
        guard_w("vector", sl_stt[sp])
        rs1 = op("vector", lambda e, par=par, b0=b0: e.tensor_reduce(
            stt[:, b0 + 4:b0 + 5], xi[:, par, :], axis=AX.X, op=ALU.add), "dve")
        note_r("dve", rs1, sl_xi[par])
        wait("vector", "act", sq1)
        rq1 = op("vector", lambda e, sp=sp, b0=b0: e.tensor_reduce(
            stt[:, b0 + 0:b0 + 1], junk3[:, sp, :], axis=AX.X, op=ALU.add), "dve")
        note_r("dve", rq1, sl_junk3[sp])
        ln_stats(sp, tiny[:, 4:5], tiny[:, 5:6], sl_tiny, sl_tiny)
        guard_w("vector", sl_xl3)
        xlv = op("vector", lambda e, par=par: e.tensor_scalar(
            xl3[:], xi[:, par, :], tiny[:, 4:5], tiny[:, 5:6],
            op0=ALU.subtract, op1=ALU.mult), "dve")
        note_r("dve", xlv, sl_xi[par], sl_tiny)
        note_w("dve", xlv, sl_xl3)
        guard_r("vector", sl_xl3, sl_xi[par])
        guard_w("vector", sl_xpx[par])
        DR()
        xpd = op("vector", lambda e, par=par: e.tensor_tensor(
            xpx[:, par, :], xi[:, par, :], xl3[:], op=ALU.add), "dve")
        note_r("dve", xpd, sl_xl3, sl_xi[par])
        note_w("dve", xpd, sl_xpx[par])
        DR()
        guard_r("tensor", sl_o2[par])
        guard_w("tensor", sl_psT3[par])
        for ecc in range(16):
            op("tensor", lambda e, par=par, ecc=ecc: e.transpose(
                psT3[par][:, ecc * 128:(ecc + 1) * 128],
                o2[:, par, ecc * 128:(ecc + 1) * 128], identb[:]))
        op("tensor", lambda e: e.drain(), "pe")
        trd = ctr["pe"]
        note_r("pe", trd, sl_o2[par])
        note_w("pe", trd, sl_psT3[par])
        wait("scalar", "pe", trd)
        guard_w("scalar", sl_o2T[par])
        evt = op("scalar", lambda e, par=par: e.activation(
            o2T[:, par, :, :], psT3[par].rearrange("p (a c) -> p a c", c=128),
            AF.Identity), "act")
        note_r("act", evt, sl_psT3[par])
        note_w("act", evt, sl_o2T[par])
        prog["scalar"].append(lambda e: e.drain())
        wait("tensor", "act", evt)
        guard_r("tensor", sl_Wob, sl_bor)
        guard_w("tensor", sl_psP3[par])
        for dch in range(2):
            op("tensor", lambda e, par=par, dch=dch: e.matmul(
                psP3[dch], lhsT=onest[:], rhs=bor[:, dch * 512:(dch + 1) * 512],
                start=True, stop=False))
            for ecc in range(16):
                op("tensor", lambda e, par=par, dch=dch, ecc=ecc: e.matmul(
                    psP3[dch], lhsT=o2T[:, par, ecc, :],
                    rhs=Wob[:, ecc, dch * 512:(dch + 1) * 512],
                    start=False, stop=(ecc == 15)),
                    "pe" if (dch == 1 and ecc == 15) else None)
        pj = ctr["pe"]
        note_w("pe", pj, sl_psP3[par])
        note_r("pe", pj, sl_o2T[par], sl_Wob, sl_bor)
        wait("vector", "pe", pj)
        guard_r("vector", sl_xpx[par])
        guard_w("vector", sl_x1t[par])
        x1d = op("vector", lambda e, par=par: e.tensor_tensor(
            x1t[:, par, :], PS(4, 2),
            xpx[:, par, :], op=ALU.add), "dve")
        note_r("dve", x1d, sl_xpx[par], sl_psP3[par])
        note_w("dve", x1d, sl_x1t[par])
        wait("scalar", "dve", x1d)
        op("scalar", lambda e, par=par, t=t: e.dma_start(
            out=x1_sc[t], in_=x1t[:, par, :]), "sst", 16)
        b3_x1st[t] = st_fl(x1_sc[t, :, 0:2])
        note_r("sst", b3_x1st[t], sl_x1t[par])
        guard_w("scalar", sl_junk3[sp])
        sq3 = op("scalar", lambda e, par=par, sp=sp: e.activation(
            junk3[:, sp, :], x1t[:, par, :], AF.Square), "act")
        note_w("act", sq3, sl_junk3[sp])
        note_r("act", sq3, sl_x1t[par])
        guard_w("vector", sl_stt[sp])
        DR()
        rs3 = op("vector", lambda e, par=par, b0=b0: e.tensor_reduce(
            stt[:, b0 + 4:b0 + 5], x1t[:, par, :], axis=AX.X, op=ALU.add), "dve")
        wait("vector", "act", sq3)
        rq3 = op("vector", lambda e, sp=sp, b0=b0: e.tensor_reduce(
            stt[:, b0 + 0:b0 + 1], junk3[:, sp, :], axis=AX.X, op=ALU.add), "dve")
        note_r("dve", rq3, sl_junk3[sp])
        ln_stats(sp, st3[:, t, 0:1], st3[:, t, 1:2], sl_st3, sl_st3)
        guard_w("vector", sl_tmp3)
        op("vector", lambda e, par=par, t=t: e.tensor_scalar(
            tmp3[:], x1t[:, par, :], st3[:, t, 0:1], None, op0=ALU.subtract))
        DR()
        op("vector", lambda e: e.tensor_reduce(
            tiny[:, 6:7], tmp3[:], axis=AX.X, op=ALU.max, apply_absolute_value=True))
        DR()
        op("vector", lambda e, t=t: e.tensor_tensor(
            tiny[:, 7:8], tiny[:, 6:7], st3[:, t, 1:2], op=ALU.mult))
        DR()
        mxd = op("vector", lambda e: e.tensor_tensor(
            mxa[:, 0:1], mxa[:, 0:1], tiny[:, 7:8], op=ALU.max), "dve")
        note_r("dve", mxd, sl_x1t[par], sl_st3)
        note_w("dve", mxd, sl_mxa, sl_tmp3, sl_tiny)
        if t + 2 < NT:
            b3_load(t + 2)

    fence()
    b3_end = {s: ctr[s] for s in ("pe", "act", "dve", "pool")}

    if debug:
        wait("gpsimd", "sst", b3_x1st[NT - 1])
        op("gpsimd", lambda e: e.dma_start(out=dbg["dbg_x1"], in_=x1_sc), "gst", 16)

    psAR = PS(6)
    sl_psAR = S()

    def emit_allreduce(idx):
        base = 16 * idx
        ccs = f"cc{idx}"
        guard_r("tensor", sl_mxa)
        guard_w("tensor", sl_psAR)
        op("tensor", lambda e: e.transpose(
            psAR[0:8, 0:128], mxa[:, 0:8], identf[:]))
        tr1 = op("tensor", lambda e: e.drain(), "pe")
        note_r("pe", tr1, sl_mxa)
        note_w("pe", tr1, sl_psAR)
        wait("vector", "pe", tr1)
        guard_w("vector", sl_tiny)
        rd = op("vector", lambda e: e.tensor_reduce(
            tiny[0:8, 0:1], psAR[0:8, 0:128], axis=AX.X, op=ALU.max), "dve")
        note_r("dve", rd, sl_psAR)
        note_w("dve", rd, sl_tiny)
        wait("gpsimd", "dve", rd)
        op("gpsimd", lambda e, idx=idx: e.dma_start(
            out=cc_in[idx], in_=tiny[0:8, 0:1]), "gst", 16)
        g0 = op("gpsimd", lambda e, idx=idx: e.dma_start(
            out=flsb[0:8, 2:3], in_=cc_in[idx]), "gst", 16)
        note_r("gst", g0, sl_tiny)
        wait("gpsimd", "gst", g0)
        op("gpsimd", lambda e, idx=idx: e.collective_compute(
            "AllReduce", ALU.max, replica_groups=[list(range(8))],
            ins=[cc_in[idx].opt()], outs=[cc_out[idx].opt()]), ccs, 1)
        wait("gpsimd", ccs, 1)
        op("gpsimd", lambda e, idx=idx: e.dma_start(
            out=tiny[0:8, 1:2], in_=cc_out[idx]), "gst", 16)
        g1 = op("gpsimd", lambda e: e.dma_start(
            out=flush_dr[0:8, 2:4], in_=tiny[0:8, 0:2]), "gst", 16)
        note_w("gst", g1, sl_tiny)
        wait("vector", "gst", g1)
        op("vector", lambda e: e.tensor_scalar(
            tiny[0:8, 2:3], tiny[0:8, 1:2], QS, 1e-12, op0=ALU.mult, op1=ALU.max))
        DR()
        sd = op("vector", lambda e: e.reciprocal(tiny[0:8, 3:4], tiny[0:8, 2:3]), "dve")
        note_w("dve", sd, sl_tiny)
        wait("tensor", "dve", sd)
        op("tensor", lambda e: e.transpose(
            psAR[0:1, 0:8], tiny[0:8, 2:3], identf[0:8, 0:8]))
        op("tensor", lambda e: e.transpose(
            psAR[0:1, 8:16], tiny[0:8, 3:4], identf[0:8, 0:8]))
        tr2 = op("tensor", lambda e: e.drain(), "pe")
        note_r("pe", tr2, sl_tiny)
        note_w("pe", tr2, sl_psAR)
        wait("vector", "pe", tr2)
        cp = op("vector", lambda e: e.tensor_copy(
            tiny[0:1, 16:32], psAR[0:1, 0:16]), "dve")
        note_r("dve", cp, sl_psAR)
        note_w("dve", cp, sl_tiny)
        wait("tensor", "dve", cp)
        guard_w("tensor", sl_psAR)
        bc = op("tensor", lambda e: e.matmul(
            psAR[:, 0:16], lhsT=onest[:], rhs=tiny[0:1, 16:32],
            start=True, stop=True), "pe")
        note_r("pe", bc, sl_tiny)
        note_w("pe", bc, sl_psAR)
        wait("vector", "pe", bc)
        guard_w("vector", sl_scal)
        fin = op("vector", lambda e, base=base: e.tensor_copy(
            scal[:, base:base + 16], psAR[:, 0:16]), "dve")
        note_r("dve", fin, sl_psAR)
        note_w("dve", fin, sl_scal)
        return fin

    ar0_done = emit_allreduce(0)

    fence()
    rw = Reg(PB)
    wdT = rw.t("wdT", [128, NIC, D], BF16)
    sl_wdT = S()
    seed(ends_now(), sl_wdT)
    PC = rw.off

    r4 = Reg(PC)
    xqT = r4.t("xqT", [128, 8, NOWN], BF16)
    x1i = r4.t("x1i", [128, 2, D], F32)
    xqn = r4.t("xqn", [128, D], I32)
    xqb = r4.t("xqb", [128, D], BF16)
    wci = r4.t("wci", [128, 2, D], F32)
    wtm = r4.t("wtm", [128, D], F32)
    wni = r4.t("wni", [128, D], I32)
    wcl = r4.t("wcl", [128, D], F32)
    wqb = r4.t("wqb", [128, 2, D], BF16)
    wTc = r4.t("wTc", [128, 2, 8, 128], BF16)
    bsx = r4.t("bsx", [128, 3, 32], F32)
    wsnap = r4.t("wsnap", [128, D], F32)
    gte = r4.t("gte", [128, 2, 512], F32)
    sl_xqT, sl_xqn, sl_xqb = S(), S(), S()
    sl_x1i, sl_wci, sl_wqb, sl_wTc, sl_gte = S(2), S(2), S(2), S(2), S(2)
    sl_wtm, sl_wni, sl_wcl, sl_bsx = S(), S(), S(), S()
    seed(ends_now(), sl_xqT, sl_xqn, sl_xqb, sl_x1i, sl_wci, sl_wqb, sl_wTc,
         sl_gte, sl_wtm, sl_wni, sl_wcl, sl_bsx)

    psX = [PS(0, 1, BF16), PS(1, 1, BF16)]
    psC = [PS(2), PS(3)]
    sl_psX, sl_psC = S(2), S(2)

    for s in ("pe", "act", "dve", "pool"):
        wait("sync", s, b3_end[s])

    for t in range(NT):
        par = t % 2
        guard_w("sync", sl_x1i[par])
        wait("sync", "sst", b3_x1st[t])
        ld_dma(x1i[:, par, :], x1_sc[t])
        xl = ld_fl(x1i[:, par, 0:2])
        note_w("ld", xl, sl_x1i[par])
        wait("vector", "ld", xl)
        wait("vector", "dve", ar0_done)
        guard_r("vector", sl_st3, sl_scal)
        guard_w("vector", sl_tiny, sl_xqn, sl_xqb)
        op("vector", lambda e, t=t: e.tensor_tensor(
            tiny[:, 8:9], st3[:, t, 1:2], scal[:, C_I1:C_I1 + 1], op=ALU.mult))
        DR()
        op("vector", lambda e, par=par, t=t: e.tensor_scalar(
            xqn[:], x1i[:, par, :], st3[:, t, 0:1], tiny[:, 8:9],
            op0=ALU.subtract, op1=ALU.mult))
        DR()
        xqd = op("vector", lambda e: e.tensor_scalar(
            xqb[:], xqn[:], scal[:, C_S1:C_S1 + 1], None, op0=ALU.mult), "dve")
        note_r("dve", xqd, sl_x1i[par], sl_st3, sl_scal)
        note_w("dve", xqd, sl_xqn, sl_xqb, sl_tiny)
        tp = t % 2
        guard_r("tensor", sl_xqb)
        guard_w("tensor", sl_psX[tp])
        for dc in range(8):
            op("tensor", lambda e, tp=tp, dc=dc: e.transpose(
                psX[tp][:, dc * 128:(dc + 1) * 128],
                xqb[:, dc * 128:(dc + 1) * 128], identb[:]))
        op("tensor", lambda e: e.drain(), "pe")
        trx = ctr["pe"]
        note_r("pe", trx, sl_xqb)
        note_w("pe", trx, sl_psX[tp])
        wait("scalar", "pe", trx)
        guard_w("scalar", sl_xqT)
        evq = op("scalar", lambda e, tp=tp, t=t: e.activation(
            xqT[:, :, t * 128:(t + 1) * 128],
            psX[tp].rearrange("p (a c) -> p a c", c=128), AF.Identity), "act")
        note_r("act", evq, sl_psX[tp])
        prog["scalar"].append(lambda e: e.drain())
    note_w("act", ctr["act"], sl_xqT)
    xq_ready = ctr["act"]

    mm_st = {}

    def c2_matmul_phase(w_src, dst_sc, mx_col, phase):
        for ic in range(NIC):
            wp = ic % 2
            guard_w("sync", sl_wci[wp])
            ld_dma(wci[:, wp, :], w_src[ic * 128:(ic + 1) * 128, :])
            lw = ld_fl(wci[:, wp, 0:2])
            note_w("ld", lw, sl_wci[wp])
            wait("vector", "ld", lw)
            if debug and phase == 0 and ic == 0:
                snp = op("vector", lambda e: e.tensor_copy(wsnap[:], wci[:, 0, :]),
                         "dve")
                prog["vector"].append(lambda e: e.drain())
                wait("gpsimd", "dve", snp)
                dsn = op("gpsimd", lambda e: e.dma_start(
                    out=dbg["dbg_wsnap"], in_=wsnap[:]), "gst", 16)
            guard_w("vector", sl_bsx)
            op("vector", lambda e, wp=wp: e.tensor_reduce(
                bsx[:, 0, :], wci[:, wp, :].rearrange("p (a b) -> p a b", b=32),
                axis=AX.X, op=ALU.max, apply_absolute_value=True))
            DR()
            op("vector", lambda e: e.tensor_scalar(
                bsx[:, 1, :], bsx[:, 0, :], 1.0 / 7.0, 1e-12, op0=ALU.mult, op1=ALU.max))
            DR()
            sd = op("vector", lambda e: e.reciprocal(bsx[:, 2, :], bsx[:, 1, :]), "dve")
            note_r("dve", sd, sl_wci[wp])
            note_w("dve", sd, sl_bsx)
            guard_r("vector", sl_wci[wp])
            guard_w("vector", sl_wtm, sl_wni, sl_wcl)
            DR()
            op("vector", lambda e, wp=wp: e.tensor_tensor(
                wtm[:].rearrange("p (a b) -> p a b", b=32),
                wci[:, wp, :].rearrange("p (a b) -> p a b", b=32),
                bsx[:, 2, :].rearrange("p (a o) -> p a o", o=1).broadcast_to(
                    [128, 32, 32]), op=ALU.mult))
            DR()
            op("vector", lambda e: e.tensor_scalar(wni[:], wtm[:], 1.0, None,
                                                   op0=ALU.mult))
            DR()
            op("vector", lambda e: e.tensor_scalar(
                wcl[:], wni[:], 7.0, -8.0, op0=ALU.min, op1=ALU.max))
            DR()
            qd = op("vector", lambda e, wp=wp: e.tensor_tensor(
                wqb[:, wp, :].rearrange("p (a b) -> p a b", b=32),
                wcl[:].rearrange("p (a b) -> p a b", b=32),
                bsx[:, 1, :].rearrange("p (a o) -> p a o", o=1).broadcast_to(
                    [128, 32, 32]), op=ALU.mult), "dve")
            note_r("dve", qd, sl_wci[wp], sl_bsx, sl_wtm, sl_wcl)
            note_w("dve", qd, sl_wqb[wp], sl_wtm, sl_wni, sl_wcl)
            DR()
            if debug and phase == 0 and ic == 0:
                wait("gpsimd", "ld", lw)
                dw1 = op("gpsimd", lambda e: e.dma_start(
                    out=dbg["dbg_wci"], in_=wci[:, 0, :]), "gst", 16)
                note_r("gst", dw1, sl_wci[0])
                wait("gpsimd", "dve", qd)
                dw2 = op("gpsimd", lambda e: e.dma_start(
                    out=dbg["dbg_wqb"], in_=wqb[:, 0, :]), "gst", 16)
                note_r("gst", dw2, sl_wqb[0])
            guard_r("tensor", sl_wqb[wp])
            guard_w("tensor", sl_psX[wp])
            for dc in range(8):
                op("tensor", lambda e, wp=wp, dc=dc: e.transpose(
                    psX[wp][:, dc * 128:(dc + 1) * 128],
                    wqb[:, wp, dc * 128:(dc + 1) * 128], identb[:]))
            op("tensor", lambda e: e.drain(), "pe")
            trw = ctr["pe"]
            note_r("pe", trw, sl_wqb[wp])
            note_w("pe", trw, sl_psX[wp])
            wait("scalar", "pe", trw)
            guard_w("scalar", sl_wTc[wp])
            evw = op("scalar", lambda e, wp=wp: e.activation(
                wTc[:, wp, :, :], psX[wp].rearrange("p (a c) -> p a c", c=128),
                AF.Identity), "act")
            note_r("act", evw, sl_psX[wp])
            note_w("act", evw, sl_wTc[wp])
            prog["scalar"].append(lambda e: e.drain())
            if debug and phase == 0 and ic == 0:
                wait("gpsimd", "act", evw)
                dw3 = op("gpsimd", lambda e: e.dma_start(
                    out=dbg["dbg_wTc"], in_=wTc[:, 0, :, :]), "gst", 16)
                note_r("gst", dw3, sl_wTc[0])
                dw4 = op("gpsimd", lambda e: e.dma_start(
                    out=dbg["dbg_xqT"], in_=xqT[:, :, 0:512]), "gst", 16)
                note_r("gst", dw4, sl_xqT)
            wait("tensor", "act", evw)
            if phase == 0 and ic == 0:
                wait("tensor", "act", xq_ready)
            for g in range(8):
                gp = (ic * 8 + g) % 2
                guard_w("tensor", sl_psC[gp])
                for dc in range(8):
                    op("tensor", lambda e, wp=wp, gp=gp, dc=dc, g=g: e.matmul(
                        psC[gp], lhsT=wTc[:, wp, dc, :],
                        rhs=xqT[:, dc, g * 512:(g + 1) * 512],
                        start=(dc == 0), stop=(dc == 7)), "pe" if dc == 7 else None)
                gmm = ctr["pe"]
                note_w("pe", gmm, sl_psC[gp])
                note_r("pe", gmm, sl_wTc[wp], sl_xqT)
                wait("scalar", "pe", gmm)
                guard_w("scalar", sl_gte[gp])
                gev = op("scalar", lambda e, gp=gp: e.activation(
                    gte[:, gp, :], psC[gp], AF.Identity), "act")
                note_r("act", gev, sl_psC[gp])
                note_w("act", gev, sl_gte[gp])
                wait("scalar", "act", gev)
                mm_st[(phase, ic * 8 + g)] = op(
                    "scalar", lambda e, gp=gp, ic=ic, g=g, dst_sc=dst_sc: e.dma_start(
                        out=dst_sc[ic, :, g * 512:(g + 1) * 512], in_=gte[:, gp, :]),
                    "sst", 16)
                note_r("sst", mm_st[(phase, ic * 8 + g)], sl_gte[gp])
                if g == 7:
                    mm_st[(phase, ic * 8 + 7)] = st_fl(dst_sc[ic, :, 0:2])
                wait("vector", "act", gev)
                guard_w("vector", sl_stt[0])
                op("vector", lambda e, gp=gp: e.tensor_reduce(
                    tiny[:, 6:7], gte[:, gp, :], axis=AX.X, op=ALU.max,
                    apply_absolute_value=True))
                DR()
                guard_w("vector", sl_mxa)
                rmx = op("vector", lambda e, mx_col=mx_col: e.tensor_tensor(
                    mxa[:, mx_col:mx_col + 1], mxa[:, mx_col:mx_col + 1],
                    tiny[:, 6:7], op=ALU.max), "dve")
                note_r("dve", rmx, sl_gte[gp])
                note_w("dve", rmx, sl_mxa, sl_tiny)

    c2_matmul_phase(w_gate, gate_sc, 2, 0)
    ar1_done = emit_allreduce(1)
    guard_w("vector", sl_scal)
    DR()
    c4d = op("vector", lambda e: e.tensor_scalar(
        scal[:, C_C4:C_C4 + 1], scal[:, C_S3:C_S3 + 1], 2.0 ** -16, None,
        op0=ALU.mult), "dve")
    note_w("dve", c4d, sl_scal)

    c2_matmul_phase(w_up, up_sc, 1, 1)
    up_end_dve = ctr["dve"]

    for dc2 in range(8):
        wp = dc2 % 2
        guard_w("sync", sl_wci[wp])
        for cs in range(3):
            guard_w("sync", sl_wci[wp])
            ld_dma(wci[:, wp, :],
                   w_down[dc2 * 128:(dc2 + 1) * 128, cs * 1024:(cs + 1) * 1024])
            lw = ld_fl(wci[:, wp, 0:2])
            note_w("ld", lw, sl_wci[wp])
            wait("vector", "ld", lw)
            if False:
                snp = op("vector", lambda e: e.tensor_copy(wsnap[:], wci[:, 0, :]),
                         "dve")
                prog["vector"].append(lambda e: e.drain())
                wait("gpsimd", "dve", snp)
                dsn = op("gpsimd", lambda e: e.dma_start(
                    out=dbg["dbg_wsnap"], in_=wsnap[:]), "gst", 16)
            guard_w("vector", sl_bsx)
            op("vector", lambda e, wp=wp: e.tensor_reduce(
                bsx[:, 0, :], wci[:, wp, :].rearrange("p (a b) -> p a b", b=32),
                axis=AX.X, op=ALU.max, apply_absolute_value=True))
            DR()
            op("vector", lambda e: e.tensor_scalar(
                bsx[:, 1, :], bsx[:, 0, :], 1.0 / 7.0, 1e-12,
                op0=ALU.mult, op1=ALU.max))
            DR()
            sd = op("vector", lambda e: e.reciprocal(bsx[:, 2, :], bsx[:, 1, :]), "dve")
            note_r("dve", sd, sl_wci[wp])
            note_w("dve", sd, sl_bsx)
            guard_r("vector", sl_wci[wp])
            guard_w("vector", sl_wtm, sl_wni, sl_wcl)
            DR()
            op("vector", lambda e, wp=wp: e.tensor_tensor(
                wtm[:].rearrange("p (a b) -> p a b", b=32),
                wci[:, wp, :].rearrange("p (a b) -> p a b", b=32),
                bsx[:, 2, :].rearrange("p (a o) -> p a o", o=1).broadcast_to(
                    [128, 32, 32]), op=ALU.mult))
            DR()
            op("vector", lambda e: e.tensor_scalar(wni[:], wtm[:], 1.0, None,
                                                   op0=ALU.mult))
            DR()
            op("vector", lambda e: e.tensor_scalar(
                wcl[:], wni[:], 7.0, -8.0, op0=ALU.min, op1=ALU.max))
            DR()
            qd = op("vector", lambda e, wp=wp: e.tensor_tensor(
                wqb[:, wp, :].rearrange("p (a b) -> p a b", b=32),
                wcl[:].rearrange("p (a b) -> p a b", b=32),
                bsx[:, 1, :].rearrange("p (a o) -> p a o", o=1).broadcast_to(
                    [128, 32, 32]), op=ALU.mult), "dve")
            note_r("dve", qd, sl_wci[wp], sl_bsx, sl_wtm, sl_wcl)
            note_w("dve", qd, sl_wqb[wp], sl_wtm, sl_wni, sl_wcl)
            DR()
            guard_r("tensor", sl_wqb[wp])
            guard_w("tensor", sl_psX[wp])
            for j in range(8):
                op("tensor", lambda e, wp=wp, j=j: e.transpose(
                    psX[wp][:, j * 128:(j + 1) * 128],
                    wqb[:, wp, j * 128:(j + 1) * 128], identb[:]))
            op("tensor", lambda e: e.drain(), "pe")
            trw = ctr["pe"]
            note_r("pe", trw, sl_wqb[wp])
            note_w("pe", trw, sl_psX[wp])
            wait("scalar", "pe", trw)
            guard_w("scalar", sl_wdT)
            evw = op("scalar", lambda e, wp=wp, cs=cs, dc2=dc2: e.activation(
                wdT[:, cs * 8:(cs + 1) * 8, dc2 * 128:(dc2 + 1) * 128],
                psX[wp].rearrange("p (a c) -> p a c", c=128), AF.Identity),
                "act")
            note_r("act", evw, sl_psX[wp])
            note_w("act", evw, sl_wdT)
            prog["scalar"].append(lambda e: e.drain())
    wd_ready = ctr["act"]

    fence()
    r5 = Reg(PC)
    gin = r5.t("gin", [128, 2, 2048], F32)
    upin = r5.t("upin", [128, 2, 2048], F32)
    tA = r5.t("tA", [128, 2048], I32)
    tB = r5.t("tB", [128, 2048], F32)
    tC = r5.t("tC", [128, 2048], F32)
    tDi = r5.t("tDi", [128, 2048], I32)
    p4t = r5.t("p4t", [128, 2, 2048], F32)
    sl_gin, sl_upin, sl_p4t = S(2), S(2), S(2)
    sl_tA, sl_tB, sl_tC, sl_tDi = S(), S(), S(), S()
    seed(ends_now(), sl_gin, sl_upin, sl_p4t, sl_tA, sl_tB, sl_tC, sl_tDi)

    NCH = NIC * 2
    c3_st = [0] * NCH

    for ch in range(NCH):
        ic, cs = ch // 2, ch % 2
        par = ch % 2
        guard_w("sync", sl_gin[par])
        wait("sync", "sst", mm_st[(0, ic * 8 + 7)])
        ld_dma(gin[:, par, :], gate_sc[ic, :, cs * 2048:(cs + 1) * 2048])
        gl = ld_fl(gin[:, par, 0:2])
        note_w("ld", gl, sl_gin[par])
        wait("vector", "ld", gl)
        if ch == 0:
            wait("vector", "dve", ar1_done)
        guard_r("vector", sl_scal)
        guard_w("vector", sl_tA)
        n3d = op("vector", lambda e, par=par: e.tensor_scalar(
            tA[:], gin[:, par, :], scal[:, C_I3:C_I3 + 1], None, op0=ALU.mult), "dve")
        note_r("dve", n3d, sl_gin[par], sl_scal)
        note_w("dve", n3d, sl_tA)
        guard_r("vector", sl_scal)
        guard_w("vector", sl_tB)
        DR()
        g1d = op("vector", lambda e: e.tensor_scalar(
            tB[:], tA[:], scal[:, C_S3:C_S3 + 1], None, op0=ALU.mult), "dve")
        note_r("dve", g1d, sl_tA, sl_scal)
        note_w("dve", g1d, sl_tB)
        wait("scalar", "dve", g1d)
        guard_w("scalar", sl_tC)
        sgd = op("scalar", lambda e: e.activation(tC[:], tB[:], AF.Sigmoid), "act")
        note_r("act", sgd, sl_tB)
        note_w("act", sgd, sl_tC)
        wait("vector", "act", sgd)
        guard_w("vector", sl_tDi, sl_tB)
        op("vector", lambda e: e.tensor_scalar(tB[:], tC[:], 65536.0, 65535.0,
                                               op0=ALU.mult, op1=ALU.min))
        DR()
        op("vector", lambda e: e.tensor_scalar(tDi[:], tB[:], 1.0, None,
                                               op0=ALU.mult))
        DR()
        op("vector", lambda e: e.tensor_tensor(tC[:], tA[:], tDi[:], op=ALU.mult))
        DR()
        guard_w("vector", sl_stt[0])
        op("vector", lambda e: e.tensor_reduce(
            tiny[:, 6:7], tC[:], axis=AX.X, op=ALU.max, apply_absolute_value=True))
        DR()
        op("vector", lambda e: e.tensor_tensor(
            tiny[:, 7:8], tiny[:, 6:7], scal[:, C_C4:C_C4 + 1], op=ALU.mult))
        DR()
        guard_w("vector", sl_mxa)
        t4d = op("vector", lambda e: e.tensor_tensor(
            mxa[:, 3:4], mxa[:, 3:4], tiny[:, 7:8], op=ALU.max), "dve")
        note_r("dve", t4d, sl_tA)
        note_w("dve", t4d, sl_tB, sl_tC, sl_tDi, sl_mxa, sl_tiny)
        guard_r("vector", sl_scal)
        guard_w("vector", sl_p4t[par])
        DR()
        p4d = op("vector", lambda e, par=par: e.tensor_scalar(
            p4t[:, par, :], tC[:], scal[:, C_C4:C_C4 + 1], None, op0=ALU.mult), "dve")
        note_r("dve", p4d, sl_tC, sl_scal)
        note_w("dve", p4d, sl_p4t[par])
        DR()
        wait("scalar", "dve", p4d)
        op("scalar", lambda e, par=par, ic=ic, cs=cs: e.dma_start(
            out=p4_sc[ic, :, cs * 2048:(cs + 1) * 2048], in_=p4t[:, par, :]), "sst", 16)
        c3_st[ch] = st_fl(p4_sc[ic, :, cs * 2048:cs * 2048 + 2])
        note_r("sst", c3_st[ch], sl_p4t[par])

    wait("vector", "dve", up_end_dve)
    ar2_done = emit_allreduce(2)
    guard_w("vector", sl_scal)
    DR()
    c45d = op("vector", lambda e: e.tensor_tensor(
        scal[:, C_C45:C_C45 + 1], scal[:, C_S4:C_S4 + 1], scal[:, C_S2:C_S2 + 1],
        op=ALU.mult), "dve")
    note_w("dve", c45d, sl_scal)

    c4_st = [0] * NCH
    for ch in range(NCH):
        ic, cs = ch // 2, ch % 2
        par = ch % 2
        guard_w("sync", sl_gin[par], sl_upin[par])
        wait("sync", "sst", c3_st[ch])
        wait("sync", "sst", mm_st[(1, ic * 8 + 7)])
        ld_dma(gin[:, par, :], p4_sc[ic, :, cs * 2048:(cs + 1) * 2048])
        ld_dma(upin[:, par, :], up_sc[ic, :, cs * 2048:(cs + 1) * 2048])
        ul = ld_fl(gin[:, par, 0:2])
        note_w("ld", ul, sl_gin[par], sl_upin[par])
        wait("vector", "ld", ul)
        if ch == 0:
            wait("vector", "dve", c45d)
        guard_r("vector", sl_scal)
        guard_w("vector", sl_tA, sl_tDi, sl_tB)
        op("vector", lambda e, par=par: e.tensor_scalar(
            tA[:], gin[:, par, :], scal[:, C_I4:C_I4 + 1], None, op0=ALU.mult))
        op("vector", lambda e, par=par: e.tensor_scalar(
            tDi[:], upin[:, par, :], scal[:, C_I2:C_I2 + 1], None, op0=ALU.mult))
        DR()
        op("vector", lambda e: e.tensor_tensor(tB[:], tA[:], tDi[:], op=ALU.mult))
        DR()
        guard_w("vector", sl_stt[0])
        op("vector", lambda e: e.tensor_reduce(
            tiny[:, 6:7], tB[:], axis=AX.X, op=ALU.max, apply_absolute_value=True))
        DR()
        op("vector", lambda e: e.tensor_tensor(
            tiny[:, 7:8], tiny[:, 6:7], scal[:, C_C45:C_C45 + 1], op=ALU.mult))
        DR()
        guard_w("vector", sl_mxa)
        t5d = op("vector", lambda e: e.tensor_tensor(
            mxa[:, 4:5], mxa[:, 4:5], tiny[:, 7:8], op=ALU.max), "dve")
        note_r("dve", t5d, sl_gin[par], sl_upin[par], sl_scal)
        note_w("dve", t5d, sl_tA, sl_tDi, sl_tB, sl_mxa, sl_tiny)
        guard_r("vector", sl_scal)
        guard_w("vector", sl_p4t[par])
        DR()
        p5d = op("vector", lambda e, par=par: e.tensor_scalar(
            p4t[:, par, :], tB[:], scal[:, C_C45:C_C45 + 1], None, op0=ALU.mult),
            "dve")
        note_r("dve", p5d, sl_tB, sl_scal)
        note_w("dve", p5d, sl_p4t[par])
        DR()
        wait("scalar", "dve", p5d)
        op("scalar", lambda e, par=par, ic=ic, cs=cs: e.dma_start(
            out=p5_sc[ic, :, cs * 2048:(cs + 1) * 2048], in_=p4t[:, par, :]), "sst", 16)
        c4_st[ch] = st_fl(p5_sc[ic, :, cs * 2048:cs * 2048 + 2])
        note_r("sst", c4_st[ch], sl_p4t[par])

    ar3_done = emit_allreduce(3)

    if debug:
        wait("gpsimd", "sst", max(c3_st))
        op("gpsimd", lambda e: e.dma_start(out=dbg["dbg_p4"], in_=p4_sc), "gst", 16)
        wait("gpsimd", "sst", max(c4_st))
        op("gpsimd", lambda e: e.dma_start(out=dbg["dbg_p5"], in_=p5_sc), "gst", 16)
        wait("gpsimd", "sst", mm_st[(0, NIC * 8 - 1)])
        op("gpsimd", lambda e: e.dma_start(out=dbg["dbg_gate"], in_=gate_sc), "gst", 16)
        wait("gpsimd", "sst", mm_st[(1, NIC * 8 - 1)])
        op("gpsimd", lambda e: e.dma_start(out=dbg["dbg_up"], in_=up_sc), "gst", 16)

    fence()
    r6 = Reg(PC)
    p5i = r6.t("p5i", [128, 2, NIC, 256], F32)
    oin = r6.t("oin", [128, NIC, 256], I32)
    ot = r6.t("ot", [128, 2, NIC, 256], BF16)
    x1c = r6.t("x1c", [128, 2, 2, D], F32)
    outt = r6.t("outt", [128, 2, 2, D], F32)
    sl_p5i, sl_ot, sl_x1c, sl_outt = S(2), S(2), S(2), S(2)
    sl_oin = S()
    seed(ends_now(), sl_p5i, sl_ot, sl_x1c, sl_outt, sl_oin)
    psF = [PS(bk) for bk in range(8)]
    sl_psF = S(2)

    NTG = 16
    c5_st = [0] * NTG
    for tg in range(NTG):
        par = tg % 2
        guard_w("sync", sl_p5i[par], sl_x1c[par])
        wait("sync", "sst", max(c4_st))
        ld_dma(p5i[:, par, :, :],
               p5_sc[:, :, tg * 256:(tg + 1) * 256].rearrange("i p t -> p i t"))
        ld_dma(x1c[:, par, :, :],
               x1_sc[tg * 2:(tg + 1) * 2].rearrange("a p d -> p a d"))
        x1l = pl = ld_fl(p5i[:, par, 0, 0:2])
        note_w("ld", pl, sl_p5i[par], sl_x1c[par])
        wait("vector", "ld", pl)
        if tg == 0:
            wait("vector", "dve", ar3_done)
        guard_r("vector", sl_scal)
        guard_w("vector", sl_oin, sl_ot[par])
        op("vector", lambda e, par=par: e.tensor_scalar(
            oin[:, :, :], p5i[:, par, :, :], scal[:, C_I5:C_I5 + 1], None,
            op0=ALU.mult))
        DR()
        od = op("vector", lambda e, par=par: e.tensor_scalar(
            ot[:, par, :, :], oin[:, :, :], scal[:, C_S5:C_S5 + 1], None,
            op0=ALU.mult), "dve")
        note_r("dve", od, sl_p5i[par], sl_scal)
        note_w("dve", od, sl_oin, sl_ot[par])
        wait("tensor", "dve", od)
        if tg == 0:
            wait("tensor", "act", wd_ready)
        guard_r("tensor", sl_wdT)
        guard_w("tensor", sl_psF[par])
        for tsub in range(2):
            for dch in range(2):
                bank = par * 4 + tsub * 2 + dch
                for icc in range(NIC):
                    op("tensor", lambda e, bank=bank, icc=icc, par=par, tsub=tsub,
                       dch=dch: e.matmul(
                        psF[bank],
                        lhsT=ot[:, par, icc, tsub * 128:(tsub + 1) * 128],
                        rhs=wdT[:, icc, dch * 512:(dch + 1) * 512],
                        start=(icc == 0), stop=(icc == NIC - 1)),
                        "pe" if icc == NIC - 1 else None)
        fm = ctr["pe"]
        note_w("pe", fm, sl_psF[par])
        note_r("pe", fm, sl_ot[par], sl_wdT)
        wait("vector", "pe", fm)
        wait("vector", "ld", x1l)
        guard_w("vector", sl_outt[par])
        for tsub in range(2):
            for dch in range(2):
                bank = par * 4 + tsub * 2 + dch
                op("vector", lambda e, bank=bank, par=par, tsub=tsub, dch=dch:
                   e.tensor_tensor(
                    outt[:, par, tsub, dch * 512:(dch + 1) * 512], psF[bank],
                    x1c[:, par, tsub, dch * 512:(dch + 1) * 512], op=ALU.add),
                    "dve" if (tsub == 1 and dch == 1) else None)
        ed = ctr["dve"]
        note_r("dve", ed, sl_psF[par], sl_x1c[par])
        note_w("dve", ed, sl_outt[par])
        wait("scalar", "dve", ed)
        c5_st[tg] = op("scalar", lambda e, par=par, tg=tg: e.dma_start(
            out=out[tg * 256:(tg + 1) * 256, :].rearrange("(a p) d -> p a d", p=128),
            in_=outt[:, par, :, :]), "sst", 16)
        note_r("sst", c5_st[tg], sl_outt[par])

    wait("sync", "sst", c5_st[NTG - 1])
    if debug:
        wait("gpsimd", "act", wd_ready)
        op("gpsimd", lambda e: e.dma_start(out=dbg["dbg_wdT"], in_=wdT[:]), "gst", 16)
        wait("gpsimd", "dve", ctr["dve"])
        op("gpsimd", lambda e: e.dma_start(out=dbg["dbg_scal"], in_=scal[:]), "gst", 16)
        op("gpsimd", lambda e: e.dma_start(out=dbg["dbg_max"], in_=mxa[:]), "gst", 16)
        wait("gpsimd", "gst", ctr["gst"])



    with ExitStack() as st_ctx:
        for s in SEMS:
            semh[s] = st_ctx.enter_context(nc.semaphore(s))
        block = st_ctx.enter_context(nc.Block())

        @block.sync
        def _(sync):
            for f in prog["sync"]:
                f(sync)

        @block.scalar
        def _(scalar):
            for f in prog["scalar"]:
                f(scalar)

        @block.vector
        def _(vector):
            for f in prog["vector"]:
                f(vector)

        @block.tensor
        def _(tensor):
            for f in prog["tensor"]:
                f(tensor)

        @block.gpsimd
        def _(gpsimd):
            for f in prog["gpsimd"]:
                f(gpsimd)

    ps_cm.__exit__(None, None, None)
    big_cm.__exit__(None, None, None)
    return nc


_NC_CACHE = {}


def _get_nc(debug=False):
    key = bool(debug)
    if key not in _NC_CACHE:
        _NC_CACHE[key] = build(key)
    return _NC_CACHE[key]


def make_in_maps(inputs):
    x = np.asarray(inputs["L0_attn_x"], np.float32)
    b_in = np.asarray(inputs["L0_attn_b_in"], np.float32)
    common = dict(
        w_in=np.ascontiguousarray(np.asarray(inputs["L0_attn_W_in"], np.float32)),
        w_out=np.ascontiguousarray(np.asarray(inputs["L0_attn_W_out"], np.float32)),
        w_gate=np.ascontiguousarray(np.asarray(inputs["L0_ffn_w_gate"], np.float32)),
        w_up=np.ascontiguousarray(np.asarray(inputs["L0_ffn_w_up"], np.float32)),
        w_down=np.ascontiguousarray(np.asarray(inputs["L0_ffn_w_down"], np.float32)),
        ln_g_rep=np.ascontiguousarray(np.broadcast_to(
            np.asarray(inputs["L0_attn_ln_g"], np.float32), (128, D))),
        ln_b_rep=np.ascontiguousarray(np.broadcast_to(
            np.asarray(inputs["L0_attn_ln_b"], np.float32), (128, D))),
        b_in_z=np.ascontiguousarray(b_in[4096:4224, None]),
        b_in_uv=np.ascontiguousarray(b_in[None, 0:4096]),
        b_out_row=np.ascontiguousarray(
            np.asarray(inputs["L0_attn_b_out"], np.float32)[None, :]),
        gq=np.ascontiguousarray(np.asarray(inputs["L0_attn_gamma_q"], np.float32)[:, None]),
        bq=np.ascontiguousarray(np.asarray(inputs["L0_attn_beta_q"], np.float32)[:, None]),
        gk=np.ascontiguousarray(np.asarray(inputs["L0_attn_gamma_k"], np.float32)[:, None]),
        bk=np.ascontiguousarray(np.asarray(inputs["L0_attn_beta_k"], np.float32)[:, None]),
        ones_f=np.ones((1, 128), np.float32),
        eps_i=np.full((128, 1), LN_EPS, np.float32),
        ident_b=np.eye(128, dtype=np.float32).astype(ml_dtypes.bfloat16),
        ident_f=np.eye(128, dtype=np.float32),
    )
    in_maps = []
    for c in range(8):
        b, h = c // 2, c % 2
        m = dict(common)
        m["x_own"] = np.ascontiguousarray(x[b, h * NOWN:(h + 1) * NOWN, :])
        m["x_peer"] = np.ascontiguousarray(x[b, (1 - h) * NOWN:(2 - h) * NOWN, :])
        in_maps.append(m)
    return in_maps


def kernel(**inputs):
    nc = _get_nc(False)
    in_maps = make_in_maps(inputs)
    res = run_bass_kernel_spmd(nc, in_maps, core_ids=list(range(8)))
    outp = np.empty((B, L, D), np.float32)
    for c in range(8):
        b, h = c // 2, c % 2
        outp[b, h * NOWN:(h + 1) * NOWN, :] = res.results[c]["out"]
    return outp
